# version 08
# baseline (speedup 1.0000x reference)
import numpy as np
import ml_dtypes
import concourse.bass as bass
import concourse.mybir as mybir
from concourse.bass_utils import run_bass_kernel_spmd

F32 = mybir.dt.float32
BF16 = mybir.dt.bfloat16
AF = mybir.ActivationFunctionType
ALU = mybir.AluOpType
AX = mybir.AxisListType

ENGINES = ("pe", "act", "dve", "sp", "gp")
DMA_ENGINES = ("sp", "gp")
GAP_MIN = 3

NCORE = 8
BLOC = 2
T = 2048
D = 1024
NT = 16
TC = 512
NCHUNK = 4
TPC = 4
H0, D0 = 16, 64
EPS_LN = 1e-5
EPS_ATT = 1e-6
NEG = -1e30


class Buf:
    __slots__ = ("name", "w", "r")

    def __init__(self, name="b"):
        self.name = name
        self.w = None
        self.r = []


class _Group:
    __slots__ = ("engine", "ops", "count", "closed")

    def __init__(self, engine):
        self.engine = engine
        self.ops = []
        self.count = None
        self.closed = False


class _Op:
    __slots__ = ("fn", "waits")

    def __init__(self, fn):
        self.fn = fn
        self.waits = {}


NSEM_DMA = 8


class Prog:

    def __init__(self, nc):
        self.nc = nc
        self.eng_groups = {e: [] for e in ENGINES}
        self.open = {e: None for e in ENGINES}
        self.count = {e: 0 for e in ENGINES}
        self.dma_seq = {e: 0 for e in DMA_ENGINES}
        self.dma_cnt = {}
        self.idx = {e: 0 for e in ENGINES}
        self.sems = {}
        self.nops = 0

    def alloc_sems(self):
        for e in ENGINES:
            if e in DMA_ENGINES:
                for j in range(NSEM_DMA):
                    self.sems[(e, j)] = self.nc.alloc_semaphore(name=f"sem_{e}{j}")
            else:
                self.sems[e] = self.nc.alloc_semaphore(name=f"sem_{e}")

    def _close(self, eng):
        g = self.open[eng]
        if g is None:
            return
        if eng in DMA_ENGINES:
            j = self.dma_seq[eng] % NSEM_DMA
            self.dma_seq[eng] += 1
            key = (eng, j)
            self.dma_cnt[key] = self.dma_cnt.get(key, 0) + 16 * len(g.ops)
            g.count = (key, self.dma_cnt[key])
        else:
            self.count[eng] += 1
            g.count = (eng, self.count[eng])
        g.closed = True
        self.open[eng] = None

    def fence(self, eng):
        self._close(eng)

    def op(self, eng, fn, r=(), w=(), serialize=False):
        o = _Op(fn)
        need = {}

        def add_need(key_cnt):
            key, cnt = key_cnt
            if need.get(key, 0) < cnt:
                need[key] = cnt

        myidx = self.idx[eng]
        hazard = eng in DMA_ENGINES or eng != "pe"
        for b in r:
            if b is None or b.w is None:
                continue
            we, wg, widx = b.w
            if we != eng:
                if not wg.closed:
                    self._close(we)
                add_need(wg.count)
            else:
                if eng in DMA_ENGINES or (hazard and (myidx - widx) <= GAP_MIN):
                    if not wg.closed:
                        self._close(we)
                    add_need(wg.count)
        for b in w:
            if b is None:
                continue
            if b.w is not None:
                we, wg, widx = b.w
                if we != eng or eng in DMA_ENGINES or (hazard and (myidx - widx) <= GAP_MIN):
                    if not wg.closed:
                        self._close(we)
                    add_need(wg.count)
            for re, rg, ridx in b.r:
                if re != eng or eng in DMA_ENGINES or (hazard and (myidx - ridx) <= GAP_MIN):
                    if not rg.closed:
                        self._close(re)
                    add_need(rg.count)
        if serialize and self.open[eng] is not None:
            self._close(eng)
            add_need((eng, self.count[eng]) if eng in DMA_ENGINES else (eng, self.count[eng]))
        if self.open[eng] is not None and any(
                (k == eng) or (isinstance(k, tuple) and k[0] == eng) for k in need):
            self._close(eng)
        o.waits = need
        g = self.open[eng]
        if g is None:
            g = _Group(eng)
            self.open[eng] = g
            self.eng_groups[eng].append(g)
        g.ops.append(o)
        self.idx[eng] += 1
        self.nops += 1
        for b in r:
            if b is not None:
                b.r.append((eng, g, myidx))
        for b in w:
            if b is not None:
                b.w = (eng, g, myidx)
                b.r = []
        return o

    def emit(self, final_engine="gp"):
        nc = self.nc
        for e in ENGINES:
            self._close(e)
        finals = {}
        for e in ENGINES:
            if e in DMA_ENGINES:
                for j in range(NSEM_DMA):
                    if self.dma_cnt.get((e, j), 0) > 0:
                        finals[(e, j)] = self.dma_cnt[(e, j)]
            elif self.count[e] > 0:
                finals[e] = self.count[e]
        eng_obj = {"pe": nc.tensor, "act": nc.scalar, "dve": nc.vector,
                   "sp": nc.sync, "gp": nc.gpsimd}

        def make(engine):
            def body(eng):
                hwm = {}

                def wait(key, val):
                    if hwm.get(key, 0) < val:
                        eng.wait_ge(self.sems[key], val)
                        hwm[key] = val

                for g in self.eng_groups[engine]:
                    n = len(g.ops)
                    if engine in DMA_ENGINES:
                        key, cnt = g.count
                        prev = cnt - 16 * n
                        if prev > 0:
                            wait(key, prev)
                    for i, o in enumerate(g.ops):
                        for se in sorted(o.waits, key=str):
                            wait(se, o.waits[se])
                        ins = o.fn(eng_obj[engine])
                        if engine in DMA_ENGINES:
                            ins.then_inc(self.sems[g.count[0]], 16)
                        elif i == n - 1:
                            ins.then_inc(self.sems[engine], 1)
                if engine == final_engine:
                    for se in sorted(finals, key=str):
                        wait(se, finals[se])
            return body

        with nc.Block() as block:
            block.tensor(make("pe"))
            block.scalar(make("act"))
            block.vector(make("dve"))
            block.sync(make("sp"))
            block.gpsimd(make("gp"))


def _lhsT_blocks(w):
    Fdim, Odim = w.shape
    nf, no = Fdim // 128, Odim // 128
    return np.ascontiguousarray(
        w.reshape(nf, 128, no, 128).transpose(2, 1, 0, 3).reshape(no, 128, nf * 128))


def _rhs_blocks(w):
    Fdim, Odim = w.shape
    return np.ascontiguousarray(w.reshape(Fdim // 128, 128, Odim))


def host_prep(inputs):
    f = np.float32
    g = {}
    WqkvT = np.ascontiguousarray(np.asarray(inputs["L0_attn_Wqkv"], f).T)
    g["wq0"] = _lhsT_blocks(WqkvT[:, 0:D])
    g["wkv0"] = _rhs_blocks(WqkvT[:, D:3 * D])
    g["wpr0"] = _rhs_blocks(np.ascontiguousarray(np.asarray(inputs["L0_attn_Wproj"], f).T))
    g["bpr0"] = np.asarray(inputs["L0_attn_bproj"], f).reshape(1, 1, D)
    g["w1"] = _lhsT_blocks(np.asarray(inputs["L0_ffn_W1"], f))
    g["b1"] = np.ascontiguousarray(np.asarray(inputs["L0_ffn_b1"], f).reshape(32, 128).T).reshape(1, 128, 32)
    g["w2"] = _rhs_blocks(np.asarray(inputs["L0_ffn_W2"], f))
    g["b2"] = np.asarray(inputs["L0_ffn_b2"], f).reshape(1, 1, D)
    for nm, key in (("wq1", "L1_attn_Wq"), ("wk1", "L1_attn_Wk"), ("wv1", "L1_attn_Wv")):
        w = np.asarray(inputs[key], f)
        g[nm] = _rhs_blocks(np.ascontiguousarray(w.transpose(1, 0, 2).reshape(D, D)))
    g["wpr1"] = _rhs_blocks(np.ascontiguousarray(np.asarray(inputs["L1_attn_proj_w"], f).T))
    g["bpr1"] = np.asarray(inputs["L1_attn_proj_b"], f).reshape(1, 1, D)
    g["wrt"] = _rhs_blocks(np.ascontiguousarray(np.asarray(inputs["L1_ffn_router_w"], f).T))
    g["brt"] = np.asarray(inputs["L1_ffn_router_bias"], f).reshape(1, 1, 32)
    gw = np.asarray(inputs["L1_ffn_gate_w"], f).transpose(0, 2, 1)
    uw = np.asarray(inputs["L1_ffn_up_w"], f).transpose(0, 2, 1)
    g["wg"] = np.concatenate([_lhsT_blocks(np.ascontiguousarray(gw[e])) for e in range(32)], 0)
    g["wu"] = np.concatenate([_lhsT_blocks(np.ascontiguousarray(uw[e])) for e in range(32)], 0)
    dw = np.asarray(inputs["L1_ffn_down_w"], f).transpose(0, 2, 1)
    g["wd"] = np.ascontiguousarray(dw)
    g["wsg"] = _lhsT_blocks(np.ascontiguousarray(np.asarray(inputs["L1_ffn_sgate_w"], f).T))
    g["wsu"] = _lhsT_blocks(np.ascontiguousarray(np.asarray(inputs["L1_ffn_sup_w"], f).T))
    g["wsd"] = _rhs_blocks(np.ascontiguousarray(np.asarray(inputs["L1_ffn_sdown_w"], f).T))
    g["ident"] = np.eye(128, dtype=f)
    g["esel"] = np.kron(np.eye(32, dtype=f), np.ones((1, 128), f))
    r = np.arange(128)
    g["mask"] = np.where(r[None, :] <= r[:, None], 0.0, NEG).astype(f)
    x = np.asarray(inputs["L0_attn_x"], f)
    shards = [np.ascontiguousarray(x[2 * c:2 * c + 2]) for c in range(NCORE)]
    return g, shards


PAIR_WEIGHTS = ["wq0", "wkv0", "wpr0", "bpr0", "w1", "w2", "b2", "wq1", "wk1",
                "wv1", "wpr1", "bpr1", "wrt", "brt", "wg", "wu", "wd", "wsg",
                "wsu", "wsd"]
RAW_WEIGHTS = ["b1"]


class KB:

    def __init__(self, shapes):
        self.nc = bass.Bass("TRN2", target_bir_lowering=False, debug=False)
        self.p = Prog(self.nc)
        self.shapes = shapes
        self.bufs = {}
        self.dram = {}
        self.pair = {}
        self._ctx = []

    def B(self, name):
        return self.bufs.setdefault(name, Buf(name))

    def declare(self):
        nc = self.nc
        self.x_in = nc.dram_tensor("x", [BLOC, T, D], F32, kind="ExternalInput").ap()
        self.out = nc.dram_tensor("out", [BLOC, T, D], F32, kind="ExternalOutput").ap()
        for name in PAIR_WEIGHTS + RAW_WEIGHTS + ["ident", "mask", "esel"]:
            sh = list(self.shapes[name])
            self.dram[name] = nc.dram_tensor(name, sh, F32, kind="ExternalInput").ap()
        for name in PAIR_WEIGHTS:
            nch, P, C = self.shapes[name]
            self.pair[name] = nc.dram_tensor(
                name + "_pr", [nch, P, 2 * C], BF16, kind="Internal").ap()

    def alloc(self):
        nc = self.nc
        cm = []

        def sb(name, shape, dt):
            g = nc.sbuf_tensor(name, shape, dt)
            t = g.__enter__()
            self._ctx.append(g)
            return t

        self.PH = sb("PH", [128, 20480], BF16)
        self.PL = sb("PL", [128, 20480], BF16)
        self.XCH = sb("XCH", [128, 4096], F32)
        self.FA = sb("FA", [128, 6144], F32)
        self.FB = sb("FB", [128, 8192], F32)
        self.WX = sb("WX", [128, 12288], BF16)
        self.WH = self.WX[:, 0:6144]
        self.WL = self.WX[:, 6144:12288]
        self.KVS = sb("KVS", [128, 4160], BF16)
        self.STAT = sb("STAT", [128, 96], F32)
        self.SM = sb("SM", [128, 64], F32)
        self.IDN = sb("IDN", [128, 128], F32)
        self.B1S = sb("B1S", [128, 32], F32)
        self.MSK = sb("MSK", [128, 128], F32)
        self.ONE = sb("ONE", [1, 128], BF16)
        self.ZR = sb("ZR", [1, 512], BF16)
        self.CST = sb("CST", [128, 4], F32)
        self.WTP = sb("WTP", [128, 1024], BF16)
        self.ESEL = sb("ESEL", [32, 4096], BF16)
        pg = nc.psum_tensor("PS", [128, 4096], F32)
        self.PS = pg.__enter__()
        self._ctx.append(pg)

    def close_ctx(self):
        for g in reversed(self._ctx):
            g.__exit__(None, None, None)

    def bank(self, i, n=1):
        return self.PS[:, 512 * i:512 * (i + 1 + (n - 1))]

    def pb(self, i, n=1):
        return [self.B(f"psbank{k}") for k in range(i, i + n)]

    def pseg(self, c0, c1):
        return [self.B(f"PH{k}") for k in range(c0 // 1024, (c1 + 1023) // 1024)]

    def fb(self, idx):
        return self.FB[:, 1024 * idx:1024 * (idx + 1)], self.B(f"FB{idx}")

    def keepalive(self, bank, n):
        wb = self.pb(bank)
        for i in range(n):
            self.p.op("pe", lambda e: e.matmul(
                self.bank(bank), self.ONE[:], self.ZR[:],
                start=True, stop=True), r=[self.B("one"), self.B("zr")], w=wb)

    def wseg(self, c0, c1):
        return [self.B(f"WH{k}") for k in range(c0 // 1024, (c1 + 1023) // 1024)]

    def smseg(self, c0, c1):
        return [self.B(f"SM{k}") for k in range(c0 // 16, (c1 + 15) // 16)]

    def faseg(self, c0, c1):
        return [self.B(f"FA{k}") for k in range(c0 // 1024, (c1 + 1023) // 1024)]

    def mm3(self, psum_ap, lh, ll, rh, rl, rbufs, wbufs, start, stop, tp=None,
            serialize=False):
        p = self.p
        p.op("pe", lambda e: e.matmul(psum_ap, lh, rh, start=start, stop=False,
                                      tile_position=tp), r=rbufs, w=wbufs,
             serialize=serialize)
        p.op("pe", lambda e: e.matmul(psum_ap, lh, rl, start=False, stop=False,
                                      tile_position=tp), r=rbufs, w=wbufs)
        p.op("pe", lambda e: e.matmul(psum_ap, ll, rh, start=False, stop=stop,
                                      tile_position=tp), r=rbufs, w=wbufs)

    def mmg(self, lh, ll, jobs, rbufs, tp=None):
        p = self.p
        for ps_ap, rh, rl, st, sp_, wb in jobs:
            p.op("pe", lambda e, ps_ap=ps_ap, rh=rh, st=st: e.matmul(
                ps_ap, lh, rh, start=st, stop=False, tile_position=tp), r=rbufs, w=wb)
            p.op("pe", lambda e, ps_ap=ps_ap, rl=rl: e.matmul(
                ps_ap, lh, rl, start=False, stop=False, tile_position=tp), r=rbufs, w=wb)
        for ps_ap, rh, rl, st, sp_, wb in jobs:
            p.op("pe", lambda e, ps_ap=ps_ap, rh=rh, sp_=sp_: e.matmul(
                ps_ap, ll, rh, start=False, stop=sp_, tile_position=tp), r=rbufs, w=wb)

    def wload(self, name, ch, whc, n, prows=128):
        pr = self.pair[name]
        C = self.shapes[name][2]
        p = self.p
        bufs = self.wseg(whc, whc + n)
        dst = self.WX[:prows].rearrange("p (h c) -> p h c", c=6144)[:, :, whc:whc + n]
        src = pr[ch].rearrange("p (h c) -> p h c", c=C)[:prows, :, 0:n]
        p.op("sp", lambda e, dst=dst, src=src: e.dma_start(out=dst, in_=src),
             r=[self.B(name + "_pair")], w=bufs)
        return bufs

    def presplit(self):
        p, nc = self.p, self.nc
        p.op("sp", lambda e: e.dma_start(out=self.IDN[:], in_=self.dram["ident"][:]),
             w=[self.B("ident")])
        p.op("sp", lambda e: e.dma_start(out=self.MSK[:], in_=self.dram["mask"][:]),
             w=[self.B("mask")])
        p.op("dve", lambda e: e.memset(self.ONE[:], 1.0), w=[self.B("one")])
        p.op("dve", lambda e: e.memset(self.ZR[:], 0.0), w=[self.B("zr")])
        p.op("dve", lambda e: e.memset(self.CST[:, 0:1], 0.125), w=[self.B("cst")])
        p.op("sp", lambda e: e.dma_start(out=self.B1S[:], in_=self.dram["b1"][0]),
             w=[self.B("b1s")])
        p.op("sp", lambda e: e.dma_start(out=self.FA[0:32, 0:4096], in_=self.dram["esel"][:]),
             w=[self.B("esel32")])
        p.op("act", lambda e: e.copy(self.ESEL[:], self.FA[0:32, 0:4096]),
             r=[self.B("esel32")], w=[self.B("esel")])
        pc = 0
        for name in PAIR_WEIGHTS:
            nch, P, C = self.shapes[name]
            pr = self.pair[name]
            for ch in range(nch):
                for c0 in range(0, C, 1024):
                    cw = min(1024, C - c0)
                    slot = pc % 2
                    pc += 1
                    fb = self.FB[:P, 1024 * slot:1024 * slot + cw]
                    wh = self.WH[:P, 1024 * slot:1024 * slot + cw]
                    wl = self.WL[:P, 1024 * slot:1024 * slot + cw]
                    bF = self.B(f"FB{slot}")
                    bW = self.wseg(1024 * slot, 1024 * slot + cw)
                    src = self.dram[name]
                    p.op("sp", lambda e, src=src, ch=ch, c0=c0, cw=cw, fb=fb:
                         e.dma_start(out=fb, in_=src[ch, :, c0:c0 + cw]), w=[bF])
                    p.op("act", lambda e, fb=fb, wh=wh: e.copy(wh, fb), r=[bF], w=bW)
                    p.op("dve", lambda e, fb=fb, wh=wh, wl=wl: e.tensor_sub(wl, fb, wh),
                         r=[bF] + bW, w=bW)
                    p.op("gp", lambda e, pr=pr, ch=ch, c0=c0, cw=cw, wh=wh:
                         e.dma_start(out=pr[ch, :, c0:c0 + cw], in_=wh), r=bW,
                         w=[self.B(name + "_pair")])
                    p.op("gp", lambda e, pr=pr, ch=ch, c0=c0, cw=cw, wl=wl, C=C:
                         e.dma_start(out=pr[ch, :, C + c0:C + c0 + cw], in_=wl), r=bW,
                         w=[self.B(name + "_pair")])

    def ln_stats_apply(self, src_ap, src_buf, stat_col, compute_stats, out32, out32_buf,
                       dump_ap=None, dump_buf=None):
        p = self.p
        S = self.STAT
        nm = S[:, stat_col:stat_col + 1]
        rs = S[:, stat_col + 1:stat_col + 2]
        bS = self.B(f"stat{stat_col}")
        if compute_stats:
            sm = self.SM
            SM0 = self.B("SM0")
            p.op("dve", lambda e: e.reduce_sum(sm[:, 0:1], src_ap, axis=AX.X),
                 r=[src_buf], w=[SM0])
            p.op("dve", lambda e: e.tensor_scalar(nm, sm[:, 0:1], -1.0 / D, None, ALU.mult),
                 r=[SM0], w=[bS])
            p.op("act", lambda e: e.activation(out=dump_ap, in_=src_ap,
                                               func=AF.Square, bias=nm,
                                               accum_out=sm[:, 1:2]),
                 r=[src_buf, bS], w=[dump_buf, SM0])
            p.op("dve", lambda e: e.tensor_scalar(sm[:, 2:3], sm[:, 1:2], 1.0 / D,
                                                  EPS_LN, ALU.mult, ALU.add),
                 r=[SM0], w=[SM0])
            p.op("act", lambda e: e.activation(out=sm[:, 3:4], in_=sm[:, 2:3], func=AF.Sqrt),
                 r=[SM0], w=[SM0])
            p.op("dve", lambda e: e.reciprocal(rs, sm[:, 3:4]), r=[SM0], w=[bS])
        p.op("dve", lambda e: e.tensor_scalar(out32, src_ap, nm, rs, ALU.add, ALU.mult),
             r=[src_buf, bS], w=[out32_buf])

    def transpose_split(self, src32, src_buf, ph_col, nch=8, pbank=5,
                        chunk_stride=128):
        p = self.p
        extent = ph_col + (chunk_stride * (nch - 1) + 128)
        pair_bufs = self.pseg(ph_col, extent)
        for half in range(0, nch, 4):
            hn = min(4, nch - half)
            pb = self.pb(pbank)[0]
            for c in range(hn):
                cc = half + c
                p.op("pe", lambda e, cc=cc, c=c: e.transpose(
                    self.PS[:, 512 * pbank + 128 * c:512 * pbank + 128 * c + 128],
                    src32[:, 128 * cc:128 * cc + 128], self.IDN[:]),
                    r=[src_buf, self.B("ident")], w=[pb])
            ps = self.PS[:, 512 * pbank:512 * pbank + 128 * hn]
            if chunk_stride == 128:
                dst_h = self.PH[:, ph_col + 128 * half:ph_col + 128 * (half + hn)]
                dst_l = self.PL[:, ph_col + 128 * half:ph_col + 128 * (half + hn)]
            else:
                c0 = ph_col + chunk_stride * half
                span = chunk_stride * hn
                dst_h = self.PH[:, c0:c0 + span].rearrange(
                    "p (c k) -> p c k", k=chunk_stride)[:, :, 0:128]
                dst_l = self.PL[:, c0:c0 + span].rearrange(
                    "p (c k) -> p c k", k=chunk_stride)[:, :, 0:128]
                ps = ps.rearrange("p (c k) -> p c k", k=128)
            p.op("act", lambda e, ps=ps, d=dst_h: e.copy(d, ps), r=[pb], w=pair_bufs)
            p.op("dve", lambda e, ps=ps, d=dst_h, dl=dst_l: e.tensor_sub(dl, ps, d),
                 r=[pb] + pair_bufs, w=pair_bufs)

    def phase_a(self, b):
        p = self.p
        PH, PL, FB, PS, WH, WL = self.PH, self.PL, self.FB, self.PS, self.WH, self.WL
        pr = self.pair["wkv0"]
        bW = self.pseg(4096, 20480)
        for ch in range(8):
            p.op("sp", lambda e, ch=ch, pr=pr: e.dma_start(
                out=PH[:, 4096 + 2048 * ch:4096 + 2048 * (ch + 1)], in_=pr[ch, :, 0:2048]),
                r=[self.B("wkv0_pair")], w=self.pseg(4096 + 2048 * ch, 4096 + 2048 * (ch + 1)))
            p.op("sp", lambda e, ch=ch, pr=pr: e.dma_start(
                out=PL[:, 4096 + 2048 * ch:4096 + 2048 * (ch + 1)], in_=pr[ch, :, 2048:4096]),
                r=[self.B("wkv0_pair")], w=self.pseg(4096 + 2048 * ch, 4096 + 2048 * (ch + 1)))
        for pt in range(2):
            vb = self.wseg(2048 + 2048 * pt, 2048 + 2048 * pt + 1040)
            p.op("dve", lambda e, pt=pt: e.memset(
                WH[:, 2048 + 2048 * pt + 64:2048 + 2048 * pt + 64 + 65 * 15 + 1:65], 1.0), w=vb)
            p.op("dve", lambda e, pt=pt: e.memset(
                WL[:, 2048 + 2048 * pt + 64:2048 + 2048 * pt + 64 + 65 * 15 + 1:65], 0.0), w=vb)
        for bk in (6, 7):
            p.op("pe", lambda e, bk=bk: e.matmul(
                self.bank(bk), self.ZR[:1, 0:128], self.ZR[:],
                start=True, stop=False), r=[self.B("zr")], w=self.pb(bk))
        for t in range(NT):
            pt = t % 2
            xa, xb = self.fb(3 * pt + 0)
            ya, yb = self.fb(3 * pt + 1)
            da, db = self.fb(3 * pt + 2)
            xtb = self.pseg(1024 * pt, 1024 * (pt + 1))
            kb = self.wseg(1024 * pt, 1024 * (pt + 1))
            vb = self.wseg(2048 + 2048 * pt, 2048 + 2048 * pt + 1040)
            p.op("sp", lambda e, t=t, xa=xa: e.dma_start(
                out=xa, in_=self.x_in[b, 128 * t:128 * (t + 1), :]), w=[xb])
            self.ln_stats_apply(xa, xb, 2 * (NT * b + t), True, ya, yb,
                                dump_ap=da, dump_buf=db)
            self.transpose_split(ya, yb, 1024 * pt, chunk_stride=128)
            for fc in range(8):
                wh = PH[:, 4096 + 2048 * fc:4096 + 2048 * (fc + 1)]
                wl = PL[:, 4096 + 2048 * fc:4096 + 2048 * (fc + 1)]
                xh = PH[:, 1024 * pt + 128 * fc:1024 * pt + 128 * (fc + 1)]
                xl = PL[:, 1024 * pt + 128 * fc:1024 * pt + 128 * (fc + 1)]
                jobs = []
                for sx in range(4):
                    jobs.append((self.bank(sx), wh[:, 512 * sx:512 * (sx + 1)],
                                 wl[:, 512 * sx:512 * (sx + 1)], fc == 0, fc == 7,
                                 self.pb(sx)))
                self.mmg(xh, xl, jobs, bW + xtb)
            p.op("dve", lambda e, da=da: e.tensor_scalar(da, self.bank(0, 2), 0.0,
                                                         None, ALU.min),
                 r=self.pb(0, 2), w=[db])
            p.op("act", lambda e, da=da, xa=xa: e.activation(out=xa, in_=da, func=AF.Exp),
                 r=[db], w=[xb])
            p.op("dve", lambda e, ya=ya, xa=xa: e.scalar_tensor_tensor(
                ya, self.bank(0, 2), 0.0, xa, ALU.max, ALU.add),
                 r=self.pb(0, 2) + [xb], w=[yb])
            p.op("act", lambda e, pt=pt, ya=ya: e.copy(WH[:, 1024 * pt:1024 * (pt + 1)], ya),
                 r=[yb], w=kb)
            p.op("dve", lambda e, pt=pt, ya=ya: e.tensor_sub(
                WL[:, 1024 * pt:1024 * (pt + 1)], ya, WH[:, 1024 * pt:1024 * (pt + 1)]),
                 r=[yb] + kb, w=kb)
            vo = 2048 + 2048 * pt
            vdst_h = WH[:, vo:vo + 65 * 16].rearrange("p (h c) -> p h c", c=65)[:, :, 0:64]
            vdst_l = WL[:, vo:vo + 65 * 16].rearrange("p (h c) -> p h c", c=65)[:, :, 0:64]
            vsrc = self.bank(2, 2).rearrange("p (h c) -> p h c", c=64)
            p.op("act", lambda e, vdst_h=vdst_h, vsrc=vsrc: e.copy(vdst_h, vsrc),
                 r=self.pb(2, 2), w=vb)
            p.op("dve", lambda e, vdst_l=vdst_l, vsrc=vsrc, vdst_h=vdst_h:
                 e.tensor_tensor(vdst_l, vsrc, vdst_h, ALU.subtract),
                 r=self.pb(2, 2) + vb, w=vb)
            for h in range(H0):
                pr, rowo = h // 2, (h % 2) * 64
                if pr < 7:
                    kvap = PS[rowo:rowo + 64, 3072 + 65 * pr:3072 + 65 * pr + 65]
                else:
                    kvap = PS[rowo:rowo + 64, 3584:3584 + 65]
                kh = WH[:, 1024 * pt + 64 * h:1024 * pt + 64 * (h + 1)]
                kl = WL[:, 1024 * pt + 64 * h:1024 * pt + 64 * (h + 1)]
                vh = WH[:, vo + 65 * h:vo + 65 * h + 65]
                vl = WL[:, vo + 65 * h:vo + 65 * h + 65]
                tp = (0, rowo) if rowo else None
                e_flag = (t == NT - 1) and (h in (12, 13, 14, 15))
                wb2 = self.pb(6 if pr < 7 else 7)
                p.op("pe", lambda e, kvap=kvap, kh=kh, vh=vh, tp=tp:
                     e.matmul(kvap, kh, vh, start=False, stop=False, tile_position=tp),
                     r=kb + vb, w=wb2, serialize=True)
                p.op("pe", lambda e, kvap=kvap, kh=kh, vl=vl, tp=tp:
                     e.matmul(kvap, kh, vl, start=False, stop=False, tile_position=tp),
                     r=kb + vb, w=wb2)
                p.op("pe", lambda e, kvap=kvap, kl=kl, vh=vh, tp=tp, st=e_flag:
                     e.matmul(kvap, kl, vh, start=False, stop=st, tile_position=tp),
                     r=kb + vb, w=wb2)
        bKVp = self.B(f"kvpair{b}")
        src_a = PS[:, 3072:3072 + 455]
        src_b = PS[:, 3584:3584 + 65]
        ka = b * 2080
        p.op("act", lambda e: e.copy(self.KVS[:, ka:ka + 455], src_a), r=self.pb(6), w=[bKVp])
        p.op("act", lambda e: e.copy(self.KVS[:, ka + 455:ka + 520], src_b), r=self.pb(7), w=[bKVp])
        p.op("dve", lambda e: e.tensor_sub(self.KVS[:, ka + 1040:ka + 1495], src_a,
                                           self.KVS[:, ka:ka + 455]), r=self.pb(6) + [bKVp], w=[bKVp])
        p.op("dve", lambda e: e.tensor_sub(self.KVS[:, ka + 1495:ka + 1560], src_b,
                                           self.KVS[:, ka + 455:ka + 520]), r=self.pb(7) + [bKVp], w=[bKVp])

    def epilogue(self, ps_ap, ps_bufs, b, tt_abs, smc=48):
        p, sm, FB = self.p, self.SM, self.FB
        xt = self.XCH[:, 1024 * (tt_abs % TPC):1024 * (tt_abs % TPC) + 1024]
        bX = self.B(f"xch{tt_abs % TPC}")
        FB1, FB2 = self.B("FB1"), self.B("FB2")
        c = smc
        SM3 = self.smseg(48, 64)
        p.op("dve", lambda e: e.reduce_sum(sm[:, c:c + 1], ps_ap, axis=AX.X),
             r=ps_bufs, w=SM3)
        p.op("dve", lambda e: e.tensor_scalar(sm[:, c + 1:c + 2], sm[:, c:c + 1],
                                              -1.0 / D, None, ALU.mult),
             r=SM3, w=SM3)
        p.op("act", lambda e: e.activation(out=FB[:, 2048:3072], in_=ps_ap, func=AF.Square,
                                           bias=sm[:, c + 1:c + 2], accum_out=sm[:, c + 2:c + 3]),
             r=ps_bufs + SM3, w=[FB2] + SM3)
        p.op("dve", lambda e: e.tensor_scalar(sm[:, c + 3:c + 4], sm[:, c + 2:c + 3],
                                              1.0 / D, EPS_LN, ALU.mult, ALU.add),
             r=SM3, w=SM3)
        p.op("act", lambda e: e.activation(out=sm[:, c + 4:c + 5], in_=sm[:, c + 3:c + 4],
                                           func=AF.Sqrt), r=SM3, w=SM3)
        p.op("dve", lambda e: e.reciprocal(sm[:, c + 5:c + 6], sm[:, c + 4:c + 5]),
             r=SM3, w=SM3)
        p.op("dve", lambda e: e.tensor_scalar(FB[:, 1024:2048], ps_ap, sm[:, c + 1:c + 2],
                                              sm[:, c + 5:c + 6], ALU.add, ALU.mult),
             r=ps_bufs + SM3, w=[FB1])
        p.op("dve", lambda e: e.tensor_add(xt, xt, FB[:, 1024:2048]), r=[FB1, bX], w=[bX])

    def s1_l0attn(self, b, ci):
        p = self.p
        PH, PL, FB, FA, PS, WH, WL = self.PH, self.PL, self.FB, self.FA, self.PS, self.WH, self.WL
        self.keepalive(0, 48)
        bXT = self.pseg(0, 4096)
        for tt in range(TPC):
            t_abs = TPC * ci + tt
            bX = self.B(f"xch{tt}")
            pt = tt % 2
            ya, yb = self.fb(3 * pt + 1)
            p.op("sp", lambda e, t_abs=t_abs, tt=tt: e.dma_start(
                out=self.XCH[:, 1024 * tt:1024 * (tt + 1)],
                in_=self.x_in[b, 128 * t_abs:128 * (t_abs + 1), :]), w=[bX])
            self.ln_stats_apply(self.XCH[:, 1024 * tt:1024 * (tt + 1)], bX,
                                2 * (NT * b + t_abs), False, ya, yb)
            self.transpose_split(ya, yb, 128 * tt, chunk_stride=512)
        for oc in range(8):
            slot = oc % 2
            bWq = self.wload("wq0", oc, 1024 * slot, 1024)
            for fc in range(8):
                self.mm3(self.bank(slot), WH[:, 1024 * slot + 128 * fc:1024 * slot + 128 * (fc + 1)],
                         WL[:, 1024 * slot + 128 * fc:1024 * slot + 128 * (fc + 1)],
                         PH[:, 512 * fc:512 * (fc + 1)], PL[:, 512 * fc:512 * (fc + 1)],
                         bWq + bXT, self.pb(slot), start=(fc == 0), stop=(fc == 7))
            xa, xb = self.fb(3 * slot + 0)
            ya, yb = self.fb(3 * slot + 1)
            da, db = self.fb(3 * slot + 2)
            p.op("dve", lambda e, slot=slot, da=da: e.tensor_scalar(
                da[:, 0:512], self.bank(slot), 0.0, None, ALU.min),
                r=self.pb(slot), w=[db])
            p.op("act", lambda e, da=da, xa=xa: e.activation(
                out=xa[:, 0:512], in_=da[:, 0:512], func=AF.Exp), r=[db], w=[xb])
            p.op("dve", lambda e, slot=slot, ya=ya, xa=xa: e.scalar_tensor_tensor(
                ya[:, 0:512], self.bank(slot), 0.0, xa[:, 0:512], ALU.max, ALU.add),
                r=self.pb(slot) + [xb], w=[yb])
            qh = PH[:, 4096 + 512 * oc:4096 + 512 * (oc + 1)]
            ql = PL[:, 4096 + 512 * oc:4096 + 512 * (oc + 1)]
            qsg = self.pseg(4096 + 512 * oc, 4096 + 512 * (oc + 1))
            p.op("act", lambda e, qh=qh, ya=ya: e.copy(qh, ya[:, 0:512]), r=[yb], w=qsg)
            p.op("dve", lambda e, qh=qh, ql=ql, ya=ya: e.tensor_sub(
                ql, ya[:, 0:512], qh), r=[yb] + qsg, w=qsg)
        bQ = self.pseg(4096, 8192)
        ka = b * 2080
        for tt in range(TPC):
            base = 2 if tt % 2 == 0 else 5
            for h in range(H0):
                rowo = (h % 2) * 64
                pcol = 512 * base + 512 * (h // 7) + 65 * (h % 7)
                kvcol = ka + (65 * (h // 2) if h // 2 < 7 else 455)
                qcol = 4096 + 512 * (h // 2) + 128 * tt
                lh = PH[rowo:rowo + 64, qcol:qcol + 128]
                ll = PL[rowo:rowo + 64, qcol:qcol + 128]
                rh = self.KVS[rowo:rowo + 64, kvcol:kvcol + 65]
                rl = self.KVS[rowo:rowo + 64, 1040 + kvcol:1040 + kvcol + 65]
                self.mm3(PS[:, pcol:pcol + 65], lh, ll, rh, rl,
                         bQ + [self.B(f"kvpair{b}")], self.pb(base + h // 7), start=True,
                         stop=True, tp=(rowo, 0) if rowo else None, serialize=True)
            for a in range(3):
                nh = 7 if a < 2 else 2
                c0 = 512 * base + 512 * a + 64
                src = PS[:, c0:c0 + 65 * (nh - 1) + 1:65]
                p.op("dve", lambda e, a=a, src=src, nh=nh: e.tensor_copy(
                    self.SM[:, 7 * a:7 * a + nh], src), r=self.pb(base + a), w=self.smseg(0, 16))
            p.op("dve", lambda e: e.tensor_scalar(self.SM[:, 16:32], self.SM[:, 0:16],
                                                  EPS_ATT, None, ALU.add),
                 r=self.smseg(0, 16), w=self.smseg(16, 32))
            p.op("dve", lambda e: e.reciprocal(self.SM[:, 32:48], self.SM[:, 16:32]),
                 r=self.smseg(16, 32), w=self.smseg(32, 48))
            fao = 1024 * (tt % 2)
            bAO = self.faseg(fao, fao + 1024)[0]
            for h in range(H0):
                pcol = 512 * base + 512 * (h // 7) + 65 * (h % 7)
                p.op("dve", lambda e, h=h, pcol=pcol, fao=fao: e.tensor_scalar(
                    FA[:, fao + 64 * h:fao + 64 * (h + 1)], PS[:, pcol:pcol + 64],
                    self.SM[:, 32 + h:33 + h], None, ALU.mult),
                    r=self.pb(base + h // 7) + self.smseg(32, 48), w=[bAO])
            self.transpose_split(FA[:, fao:fao + 1024], bAO, 8192 + 1024 * tt,
                                 chunk_stride=128, pbank=tt % 2)
        bB = self.wload("bpr0", 0, 4096, 1024, prows=1)
        for tt in range(TPC):
            pbi = 2 if tt % 2 == 0 else 5
            bSrc = self.pseg(8192 + 1024 * tt, 8192 + 1024 * (tt + 1))
            for fc in range(8):
                slot = fc % 2
                bWp = self.wload("wpr0", fc, 2048 + 1024 * slot, 1024)
                jobs = [(self.bank(pbi + sx),
                         WH[:, 2048 + 1024 * slot + 512 * sx:2048 + 1024 * slot + 512 * (sx + 1)],
                         WL[:, 2048 + 1024 * slot + 512 * sx:2048 + 1024 * slot + 512 * (sx + 1)],
                         fc == 0, False, self.pb(pbi + sx)) for sx in range(2)]
                self.mmg(PH[:, 8192 + 1024 * tt + 128 * fc:8192 + 1024 * tt + 128 * (fc + 1)],
                         PL[:, 8192 + 1024 * tt + 128 * fc:8192 + 1024 * tt + 128 * (fc + 1)],
                         jobs, bWp + bSrc)
            for sx in range(2):
                p.op("pe", lambda e, pbi=pbi, sx=sx: e.matmul(
                    self.bank(pbi + sx), self.ONE[:], WH[:1, 4096 + 512 * sx:4096 + 512 * (sx + 1)],
                    start=False, stop=False), r=[self.B("one")] + bB, w=self.pb(pbi + sx))
                p.op("pe", lambda e, pbi=pbi, sx=sx: e.matmul(
                    self.bank(pbi + sx), self.ONE[:], WL[:1, 4096 + 512 * sx:4096 + 512 * (sx + 1)],
                    start=False, stop=True), r=[self.B("one")] + bB, w=self.pb(pbi + sx))
            self.epilogue(self.bank(pbi, 2), self.pb(pbi, 2), b, TPC * ci + tt)

    def s5_store(self, b, ci):
        p = self.p
        for tt in range(TPC):
            t_abs = TPC * ci + tt
            bX = self.B(f"xch{tt}")
            p.op("gp", lambda e, t_abs=t_abs, tt=tt: e.dma_start(
                out=self.out[b, 128 * t_abs:128 * (t_abs + 1), :],
                in_=self.XCH[:, 1024 * tt:1024 * (tt + 1)]), r=[bX], w=[self.B("outd")])

    def s2_ffn(self, b, ci):
        p = self.p
        PH, PL, FB, WH, WL = self.PH, self.PL, self.FB, self.WH, self.WL
        self.keepalive(0, 48)
        bXT = self.pseg(0, 4096)
        for tt in range(TPC):
            bX = self.B(f"xch{tt}")
            pt = tt % 2
            ya, yb = self.fb(3 * pt + 1)
            da, db = self.fb(3 * pt + 2)
            self.ln_stats_apply(self.XCH[:, 1024 * tt:1024 * (tt + 1)], bX,
                                64, True, ya, yb, dump_ap=da, dump_buf=db)
            self.transpose_split(ya, yb, 128 * tt, chunk_stride=512)
        wbufs1 = {0: self.wload("w1", 0, 0, 1024)}
        for hc in range(32):
            slot = hc % 2
            if hc + 1 < 32:
                wbufs1[hc + 1] = self.wload("w1", hc + 1, 1024 * ((hc + 1) % 2), 1024)
            bW1 = wbufs1.pop(hc)
            for fc in range(8):
                self.mm3(self.bank(slot),
                         WH[:, 1024 * slot + 128 * fc:1024 * slot + 128 * (fc + 1)],
                         WL[:, 1024 * slot + 128 * fc:1024 * slot + 128 * (fc + 1)],
                         PH[:, 512 * fc:512 * (fc + 1)], PL[:, 512 * fc:512 * (fc + 1)],
                         bW1 + bXT, self.pb(slot), start=(fc == 0), stop=(fc == 7))
            xa, xb = self.fb(3 * slot + 0)
            hseg = self.pseg(4096 + 512 * hc, 4096 + 512 * (hc + 1))
            p.op("act", lambda e, slot=slot, xa=xa, hc=hc: e.activation(
                out=xa[:, 0:512], in_=self.bank(slot), func=AF.Relu,
                bias=self.B1S[:, hc:hc + 1]),
                r=self.pb(slot) + [self.B("b1s")], w=[xb])
            p.op("act", lambda e, xa=xa, hc=hc: e.copy(
                PH[:, 4096 + 512 * hc:4096 + 512 * (hc + 1)], xa[:, 0:512]),
                r=[xb], w=hseg)
            p.op("dve", lambda e, xa=xa, hc=hc: e.tensor_sub(
                PL[:, 4096 + 512 * hc:4096 + 512 * (hc + 1)], xa[:, 0:512],
                PH[:, 4096 + 512 * hc:4096 + 512 * (hc + 1)]),
                r=[xb] + hseg, w=hseg)
        wbufs2 = {0: self.wload("w2", 0, 2048, 1024)}
        for hc in range(32):
            slot = 2 + hc % 2
            if hc + 1 < 32:
                wbufs2[hc + 1] = self.wload("w2", hc + 1, 2048 + 1024 * ((hc + 1) % 2), 1024)
            bW2 = wbufs2.pop(hc)
            hseg = self.pseg(4096 + 512 * hc, 4096 + 512 * (hc + 1))
            for tt in range(TPC):
                hcol = 4096 + 512 * hc + 128 * tt
                jobs = [(self.bank(2 * tt + sx),
                         WH[:, 1024 * slot + 512 * sx:1024 * slot + 512 * (sx + 1)],
                         WL[:, 1024 * slot + 512 * sx:1024 * slot + 512 * (sx + 1)],
                         hc == 0, False, self.pb(2 * tt + sx)) for sx in range(2)]
                self.mmg(PH[:, hcol:hcol + 128], PL[:, hcol:hcol + 128], jobs, bW2 + hseg)
        bB = self.wload("b2", 0, 4096, 1024, prows=1)
        for tt in range(TPC):
            for sx in range(2):
                p.op("pe", lambda e, tt=tt, sx=sx: e.matmul(
                    self.bank(2 * tt + sx), self.ONE[:],
                    WH[:1, 4096 + 512 * sx:4096 + 512 * (sx + 1)],
                    start=False, stop=False), r=[self.B("one")] + bB,
                    w=self.pb(2 * tt + sx))
                p.op("pe", lambda e, tt=tt, sx=sx: e.matmul(
                    self.bank(2 * tt + sx), self.ONE[:],
                    WL[:1, 4096 + 512 * sx:4096 + 512 * (sx + 1)],
                    start=False, stop=True), r=[self.B("one")] + bB,
                    w=self.pb(2 * tt + sx))
            self.epilogue(self.bank(2 * tt, 2), self.pb(2 * tt, 2), b, TPC * ci + tt)

    def s3_l1attn(self, b, ci):
        p = self.p
        PH, PL, FB, FA, PS, WH, WL = self.PH, self.PL, self.FB, self.FA, self.PS, self.WH, self.WL
        self.keepalive(0, 48)
        SEG0 = self.pseg(0, 4096)
        for j in range(TPC):
            bX = self.B(f"xch{j}")
            pt = j % 2
            ya, yb = self.fb(3 * pt + 1)
            da, db = self.fb(3 * pt + 2)
            self.ln_stats_apply(self.XCH[:, 1024 * j:1024 * (j + 1)], bX,
                                64, True, ya, yb, dump_ap=da, dump_buf=db)
            self.transpose_split(ya, yb, 1024 * j, chunk_stride=128)

        def headsplit(j, pbank, dst_c, dst_bufs, to_pairs, src_fb):
            rowo = 64 * (j % 2)
            sap, sbuf = self.fb(src_fb)
            for hb in range(4):
                bk = pbank + (hb % 2 if pbank % 2 == 0 else 0)
                if pbank % 2 == 0:
                    bk = pbank + hb % 2
                else:
                    bk = pbank if hb % 2 == 0 else pbank - 1
                base = 512 * bk
                pbt = self.pb(bk)
                for c in range(4):
                    h = 4 * hb + c
                    p.op("pe", lambda e, h=h, c=c, base=base, sap=sap: e.transpose(
                        PS[0:64, base + 128 * c:base + 128 * (c + 1)],
                        sap[:, 64 * h:64 * (h + 1)], self.IDN[:]),
                        r=[sbuf, self.B("ident")], w=pbt)
                src = PS[0:64, base:base + 512].rearrange("p (c t) -> p c t", t=128)
                if to_pairs:
                    dh = PH[rowo:rowo + 64, dst_c:dst_c + 2048].rearrange(
                        "p (t c) -> p c t", c=16)[:, 4 * hb:4 * hb + 4, :]
                    dl = PL[rowo:rowo + 64, dst_c:dst_c + 2048].rearrange(
                        "p (t c) -> p c t", c=16)[:, 4 * hb:4 * hb + 4, :]
                    p.op("act", lambda e, src=src, dh=dh: e.copy(dh, src),
                         r=pbt, w=dst_bufs)
                    p.op("dve", lambda e, src=src, dh=dh, dl=dl:
                         e.tensor_sub(dl, src, dh), r=pbt + dst_bufs, w=dst_bufs)
                else:
                    dst = FB[0:64, 2048:4096].rearrange(
                        "p (t c) -> p c t", c=16)[:, 4 * hb:4 * hb + 4, :]
                    p.op("dve", lambda e, src=src, dst=dst: e.tensor_copy(dst, src),
                         r=pbt, w=[self.fb(2)[1], self.fb(3)[1]])

        for wi, wname in ((0, "wq1"), (1, "wk1")):
            for fc in range(8):
                slot = fc % 2
                bW = self.wload(wname, fc, 1024 * slot, 1024)
                for j in range(TPC):
                    jobs = [(self.bank(2 * j + sx),
                             WH[:, 1024 * slot + 512 * sx:1024 * slot + 512 * (sx + 1)],
                             WL[:, 1024 * slot + 512 * sx:1024 * slot + 512 * (sx + 1)],
                             fc == 0, fc == 7, self.pb(2 * j + sx)) for sx in range(2)]
                    self.mmg(PH[:, 1024 * j + 128 * fc:1024 * j + 128 * (fc + 1)],
                             PL[:, 1024 * j + 128 * fc:1024 * j + 128 * (fc + 1)],
                             jobs, bW + SEG0)
            for j in range(TPC):
                dst_c = 4096 + 4096 * (j // 2) + 2048 * wi
                dstb = self.pseg(dst_c, dst_c + 2048)
                sfb = 4 + j % 2
                sap, sbuf = self.fb(sfb)
                p.op("dve", lambda e, j=j, sap=sap: e.tensor_copy(
                    sap, self.bank(2 * j, 2)), r=self.pb(2 * j, 2), w=[sbuf])
                headsplit(j, 2 * j, dst_c, dstb, True, sfb)

        def vproj(j):
            for sx in range(2):
                for fc in range(8):
                    slot = fc % 2
                    bW = self.wload("wv1", fc, 1024 * slot, 1024)
                    self.mm3(self.bank(7),
                             PH[:, 1024 * j + 128 * fc:1024 * j + 128 * (fc + 1)],
                             PL[:, 1024 * j + 128 * fc:1024 * j + 128 * (fc + 1)],
                             WH[:, 1024 * slot + 512 * sx:1024 * slot + 512 * (sx + 1)],
                             WL[:, 1024 * slot + 512 * sx:1024 * slot + 512 * (sx + 1)],
                             bW + SEG0, self.pb(7), start=(fc == 0), stop=(fc == 7))
                p.op("dve", lambda e, sx=sx: e.tensor_copy(
                    FB[:, 4096 + 512 * sx:4096 + 512 * (sx + 1)], self.bank(7)),
                    r=self.pb(7), w=[self.fb(4)[1]])

        def vtail(j):
            headsplit(j, 4, 0, None, False, 4)
            for wv in range(2):
                bk = 4 + wv % 2
                pbt = self.pb(bk)
                for c in range(8):
                    cc = 8 * wv + c
                    p.op("pe", lambda e, cc=cc, c=c, bk=bk: e.transpose(
                        PS[:, 512 * bk + 64 * c:512 * bk + 64 * (c + 1)],
                        FB[0:64, 2048 + 128 * cc:2048 + 128 * (cc + 1)],
                        self.IDN[0:64, 0:64]),
                        r=[self.fb(2)[1], self.fb(3)[1], self.B("ident")], w=pbt)
                src = PS[:, 512 * bk:512 * bk + 512].rearrange("p (c d) -> p c d", d=64)
                for half in range(2):
                    dst = FB[:, 0:2048].rearrange("p (c d) -> p c d", d=128)[
                        :, 8 * wv:8 * wv + 8, 64 * half:64 * half + 64]
                    p.op("dve", lambda e, src=src, dst=dst:
                         e.tensor_copy(dst, src), r=pbt,
                         w=[self.fb(0)[1], self.fb(1)[1]])

        vproj(0)
        vtail(0)
        for j in range(TPC):
            self._s3_attn_core(b, ci, j)
            if j + 1 < TPC:
                vproj(j + 1)
                vtail(j + 1)
        for fc in range(8):
            slot = fc % 2
            bWp = self.wload("wpr1", fc, 1024 * slot, 1024)
            for j in range(TPC):
                aoc = 12288 + 1024 * j
                bSrc = self.pseg(aoc, aoc + 1024)
                jobs = [(self.bank(2 * j + sx),
                         WH[:, 1024 * slot + 512 * sx:1024 * slot + 512 * (sx + 1)],
                         WL[:, 1024 * slot + 512 * sx:1024 * slot + 512 * (sx + 1)],
                         fc == 0, False, self.pb(2 * j + sx)) for sx in range(2)]
                self.mmg(PH[:, aoc + 128 * fc:aoc + 128 * (fc + 1)],
                         PL[:, aoc + 128 * fc:aoc + 128 * (fc + 1)],
                         jobs, bWp + bSrc)
        bB = self.wload("bpr1", 0, 2048, 1024, prows=1)
        for j in range(TPC):
            for sx in range(2):
                p.op("pe", lambda e, j=j, sx=sx: e.matmul(
                    self.bank(2 * j + sx), self.ONE[:], WH[:1, 2048 + 512 * sx:2048 + 512 * (sx + 1)],
                    start=False, stop=False), r=[self.B("one")] + bB, w=self.pb(2 * j + sx))
                p.op("pe", lambda e, j=j, sx=sx: e.matmul(
                    self.bank(2 * j + sx), self.ONE[:], WL[:1, 2048 + 512 * sx:2048 + 512 * (sx + 1)],
                    start=False, stop=True), r=[self.B("one")] + bB, w=self.pb(2 * j + sx))
            self.epilogue(self.bank(2 * j, 2), self.pb(2 * j, 2), b, TPC * ci + j)

    def _s3_attn_core(self, b, ci, j):
        p = self.p
        PH, PL, FB, FA, PS, WH, WL = self.PH, self.PL, self.FB, self.FA, self.PS, self.WH, self.WL
        rowo = 64 * (j % 2)
        QTc = 4096 + 4096 * (j // 2)
        KTc = QTc + 2048
        bQT = self.pseg(QTc, QTc + 2048)
        bKT = self.pseg(KTc, KTc + 2048)
        bAe = self.faseg(0, 2048)
        bAo = [self.fb(2)[1], self.fb(3)[1]]
        bO = self.faseg(2048, 4096)
        bAT = self.faseg(4096, 6144)
        bVD = [self.fb(0)[1], self.fb(1)[1]]
        SM0 = self.smseg(0, 16)
        tp = (rowo, 0) if rowo else None

        APAR = [(FA[:, 0:2048], bAe), (FB[:, 2048:4096], bAo),
                (FB[:, 6144:8192], [self.fb(6)[1], self.fb(7)[1]])]

        def at_av(i):
            Aap, bA = APAR[i % 3]
            for c0 in range(0, i + 1, 4):
                cn = min(4, i + 1 - c0)
                bk = 4 + (c0 // 4) % 2
                pbt = self.pb(bk)
                for c in range(cn):
                    p.op("pe", lambda e, c=c, c0=c0, Aap=Aap, bk=bk: e.transpose(
                        PS[:, 512 * bk + 128 * c:512 * bk + 128 * (c + 1)],
                        Aap[:, 128 * (c0 + c):128 * (c0 + c + 1)], self.IDN[:]),
                        r=bA + [self.B("ident")], w=pbt)
                p.op("dve", lambda e, c0=c0, cn=cn, bk=bk: e.tensor_copy(
                    FA[:, 4096 + 128 * c0:4096 + 128 * (c0 + cn)],
                    PS[:, 512 * bk:512 * bk + 128 * cn]), r=pbt, w=bAT)
            ob = 6 + i % 2
            for c in range(i + 1):
                p.op("pe", lambda e, c=c, i=i, ob=ob: e.matmul(
                    self.bank(ob)[:, 0:128], FB[:, 128 * c:128 * (c + 1)],
                    FA[:, 4096 + 128 * c:4096 + 128 * (c + 1)],
                    start=(c == 0), stop=(c == i)), r=bVD + bAT, w=self.pb(ob))
            p.op("dve", lambda e, i=i, ob=ob: e.tensor_copy(
                FA[:, 2048 + 128 * i:2048 + 128 * (i + 1)], self.bank(ob)[:, 0:128]),
                r=self.pb(ob), w=bO)

        for i in range(16):
            if i > 1:
                at_av(i - 2)
            nk = 128 * (i + 1)
            nb = (nk + 511) // 512
            qh = PH[rowo:rowo + 64, QTc + 128 * i:QTc + 128 * (i + 1)]
            ql = PL[rowo:rowo + 64, QTc + 128 * i:QTc + 128 * (i + 1)]
            jobs = []
            for c0 in range(0, nk, 512):
                cw = min(512, nk - c0)
                jobs.append((PS[:, c0:c0 + cw],
                             PH[rowo:rowo + 64, KTc + c0:KTc + c0 + cw],
                             PL[rowo:rowo + 64, KTc + c0:KTc + c0 + cw],
                             True, True, self.pb(c0 // 512)))
            self.mmg(qh, ql, jobs, bQT + bKT, tp=tp)
            p.op("dve", lambda e, nk=nk: e.tensor_tensor(
                PS[:, nk - 128:nk], PS[:, nk - 128:nk], self.MSK[:], ALU.add),
                r=self.pb((nk - 128) // 512) + [self.B("mask")], w=self.pb((nk - 128) // 512))
            p.op("dve", lambda e, nk=nk: e.reduce_max(
                self.SM[:, 0:1], PS[:, 0:nk], axis=AX.X, negate=True),
                r=self.pb(0, nb), w=[SM0[0]])
            p.op("dve", lambda e: e.tensor_scalar(self.SM[:, 1:2], self.SM[:, 0:1],
                                                  0.125, None, ALU.mult),
                 r=[SM0[0]], w=[SM0[0]])
            Aap, bA = APAR[i % 3]
            p.op("act", lambda e, nk=nk, Aap=Aap: e.activation(
                out=Aap[:, 0:nk], in_=PS[:, 0:nk], func=AF.Exp,
                bias=self.SM[:, 1:2], scale=self.CST[:, 0:1], accum_out=self.SM[:, 2:3]),
                r=self.pb(0, nb) + [SM0[0], self.B("cst")], w=bA + [SM0[0]])
            p.op("dve", lambda e: e.reciprocal(self.SM[:, 3:4], self.SM[:, 2:3]),
                 r=[SM0[0]], w=[SM0[0]])
            p.op("act", lambda e, nk=nk, Aap=Aap: e.activation(
                out=Aap[:, 0:nk], in_=Aap[:, 0:nk], func=AF.Copy,
                scale=self.SM[:, 3:4]), r=bA + [SM0[0]], w=bA)
        at_av(14)
        at_av(15)
        aoc = 12288 + 1024 * j
        bO = self.faseg(2048, 4096)
        bAO = self.pseg(aoc, aoc + 1024)
        for c in range(8):
            for half in range(2):
                rr = slice(64 * half, 64 * (half + 1))
                src = FA[rr, 2048 + 2 * c + half:2048 + 2 * c + half + 16 * 127 + 1:16]
                dh = PH[rr, aoc + 128 * c:aoc + 128 * (c + 1)]
                dl = PL[rr, aoc + 128 * c:aoc + 128 * (c + 1)]
                p.op("act", lambda e, src=src, dh=dh: e.copy(dh, src), r=bO, w=bAO)
                p.op("dve", lambda e, src=src, dh=dh, dl=dl: e.tensor_sub(dl, src, dh),
                     r=bO + bAO, w=bAO)

    def s4_moe(self, b, ci):
        p = self.p
        PH, PL, FB, FA, PS, WH, WL = self.PH, self.PL, self.FB, self.FA, self.PS, self.WH, self.WL
        self.keepalive(6, 48)
        SEG0 = self.pseg(0, 4096)
        SM0 = self.smseg(0, 16)
        bFA = self.faseg(0, 1024)[0]
        bWT = self.B("wtp")
        for tt in range(TPC):
            bX = self.B(f"xch{tt}")
            pt = tt % 2
            ya, yb = self.fb(3 * pt + 1)
            da, db = self.fb(3 * pt + 2)
            self.ln_stats_apply(self.XCH[:, 1024 * tt:1024 * (tt + 1)], bX,
                                64, True, ya, yb, dump_ap=da, dump_buf=db)
            self.transpose_split(ya, yb, 128 * tt, chunk_stride=512)
        bB = self.wload("brt", 0, 2048, 32, prows=1)
        for tt in range(TPC):
            bk = 4 + tt
            for fc in range(8):
                bW = self.wload("wrt", fc, 1024 * (fc % 2), 32)
                self.mm3(PS[:, 512 * bk:512 * bk + 32],
                         PH[:, 512 * fc + 128 * tt:512 * fc + 128 * (tt + 1)],
                         PL[:, 512 * fc + 128 * tt:512 * fc + 128 * (tt + 1)],
                         WH[:, 1024 * (fc % 2):1024 * (fc % 2) + 32],
                         WL[:, 1024 * (fc % 2):1024 * (fc % 2) + 32],
                         bW + SEG0, self.pb(bk), start=(fc == 0), stop=False)
            p.op("pe", lambda e, bk=bk: e.matmul(PS[:, 512 * bk:512 * bk + 32], self.ONE[:],
                                                 WH[:1, 2048:2048 + 32], start=False,
                                                 stop=False), r=[self.B("one")] + bB,
                 w=self.pb(bk))
            p.op("pe", lambda e, bk=bk: e.matmul(PS[:, 512 * bk:512 * bk + 32], self.ONE[:],
                                                 WL[:1, 2048:2048 + 32], start=False,
                                                 stop=True), r=[self.B("one")] + bB,
                 w=self.pb(bk))
        for tt in range(TPC):
            bk = 4 + tt
            sm = self.SM
            fo = 128 * tt
            smx = FA[:, fo:fo + 32]
            wk = FA[:, fo + 32:fo + 64]
            msk = FA[:, fo + 64:fo + 96]
            w_ = FA[:, fo + 96:fo + 128]
            lg = PS[:, 512 * bk:512 * bk + 32]
            p.op("dve", lambda e, lg=lg: e.reduce_max(sm[:, 0:1], lg, axis=AX.X,
                                                      negate=True), r=self.pb(bk), w=[SM0[0]])
            p.op("act", lambda e, lg=lg, smx=smx: e.activation(
                out=smx, in_=lg, func=AF.Exp, bias=sm[:, 0:1], accum_out=sm[:, 1:2]),
                r=self.pb(bk) + [SM0[0]], w=[bFA, SM0[0]])
            p.op("dve", lambda e: e.reciprocal(sm[:, 2:3], sm[:, 1:2]), r=[SM0[0]], w=[SM0[0]])
            p.op("dve", lambda e, smx=smx: e.tensor_scalar(smx, smx, sm[:, 2:3], None, ALU.mult),
                 r=[bFA, SM0[0]], w=[bFA])
            p.op("dve", lambda e, wk=wk, smx=smx: e.tensor_copy(wk, smx), r=[bFA], w=[bFA])
            for it in range(3):
                p.op("dve", lambda e, wk=wk: e.reduce_max(sm[:, 3:4], wk, axis=AX.X),
                     r=[bFA], w=[SM0[0]])
                p.op("dve", lambda e, wk=wk, msk=msk: e.tensor_scalar(
                    msk, wk, sm[:, 3:4], None, ALU.is_lt), r=[bFA, SM0[0]], w=[bFA])
                p.op("dve", lambda e, wk=wk, msk=msk: e.tensor_mul(wk, wk, msk),
                     r=[bFA], w=[bFA])
            p.op("dve", lambda e, wk=wk: e.reduce_max(sm[:, 4:5], wk, axis=AX.X),
                 r=[bFA], w=[SM0[0]])
            p.op("dve", lambda e, smx=smx, msk=msk: e.tensor_scalar(
                msk, smx, sm[:, 4:5], None, ALU.is_ge), r=[bFA, SM0[0]], w=[bFA])
            p.op("dve", lambda e, w_=w_, smx=smx, msk=msk: e.tensor_mul(w_, smx, msk),
                 r=[bFA], w=[bFA])
            p.op("dve", lambda e, w_=w_: e.reduce_sum(sm[:, 5:6], w_, axis=AX.X),
                 r=[bFA], w=[SM0[0]])
            p.op("dve", lambda e: e.tensor_scalar(sm[:, 6:7], sm[:, 5:6], 1e-12, None,
                                                  ALU.max), r=[SM0[0]], w=[SM0[0]])
            p.op("dve", lambda e: e.reciprocal(sm[:, 7:8], sm[:, 6:7]), r=[SM0[0]], w=[SM0[0]])
            p.op("dve", lambda e, w_=w_: e.tensor_scalar(w_, w_, sm[:, 7:8], None, ALU.mult),
                 r=[bFA, SM0[0]], w=[bFA])
            pbt = self.pb(bk)
            p.op("pe", lambda e, w_=w_, bk=bk: e.transpose(
                PS[0:32, 512 * bk + 128:512 * bk + 256], w_, self.IDN[:]),
                r=[bFA, self.B("ident")], w=pbt)
            p.op("act", lambda e, tt=tt, bk=bk: e.copy(
                self.WTP[0:32, 128 * tt:128 * (tt + 1)],
                PS[0:32, 512 * bk + 128:512 * bk + 256]), r=pbt, w=[bWT])
            p.op("dve", lambda e, tt=tt, bk=bk: e.tensor_sub(
                self.WTP[0:32, 512 + 128 * tt:512 + 128 * (tt + 1)],
                PS[0:32, 512 * bk + 128:512 * bk + 256],
                self.WTP[0:32, 128 * tt:128 * (tt + 1)]), r=pbt + [bWT], w=[bWT])
        for sc in range(4):
            pk = 2 * (sc % 2)
            bG = self.wload("wsg", sc, 0, 1024)
            bU = self.wload("wsu", sc, 1024, 1024)
            for fc in range(8):
                self.mm3(self.bank(pk), WH[:, 128 * fc:128 * (fc + 1)],
                         WL[:, 128 * fc:128 * (fc + 1)],
                         PH[:, 512 * fc:512 * (fc + 1)], PL[:, 512 * fc:512 * (fc + 1)],
                         bG + SEG0, self.pb(pk), start=(fc == 0), stop=(fc == 7))
                self.mm3(self.bank(pk + 1), WH[:, 1024 + 128 * fc:1024 + 128 * (fc + 1)],
                         WL[:, 1024 + 128 * fc:1024 + 128 * (fc + 1)],
                         PH[:, 512 * fc:512 * (fc + 1)], PL[:, 512 * fc:512 * (fc + 1)],
                         bU + SEG0, self.pb(pk + 1), start=(fc == 0), stop=(fc == 7))
            xa, xb = self.fb(3 * (sc % 2) + 0)
            p.op("act", lambda e, pk=pk, xa=xa: e.activation(out=xa[:, 0:512], in_=self.bank(pk),
                                                             func=AF.Silu), r=self.pb(pk), w=[xb])
            p.op("dve", lambda e, pk=pk, xa=xa: e.tensor_mul(xa[:, 512:1024], xa[:, 0:512],
                                                             self.bank(pk + 1)),
                 r=[xb] + self.pb(pk + 1), w=[xb])
            hsb = self.wseg(4096 + 512 * sc, 4096 + 512 * (sc + 1))
            p.op("act", lambda e, sc=sc, xa=xa: e.copy(WH[:, 4096 + 512 * sc:4096 + 512 * (sc + 1)],
                                                       xa[:, 512:1024]), r=[xb], w=hsb)
            p.op("dve", lambda e, sc=sc, xa=xa: e.tensor_sub(
                WL[:, 4096 + 512 * sc:4096 + 512 * (sc + 1)], xa[:, 512:1024],
                WH[:, 4096 + 512 * sc:4096 + 512 * (sc + 1)]), r=[xb] + hsb, w=hsb)
        wbg = {0: self.wload("wg", 0, 0, 1024)}
        wbu = {0: self.wload("wu", 0, 2048, 1024)}
        for e_ in range(32):
            slot = e_ % 2
            pk = 3 * slot
            if e_ + 1 < 32:
                wbg[e_ + 1] = self.wload("wg", e_ + 1, 1024 * ((e_ + 1) % 2), 1024)
                wbu[e_ + 1] = self.wload("wu", e_ + 1, 2048 + 1024 * ((e_ + 1) % 2), 1024)
            bG = wbg.pop(e_)
            bU = wbu.pop(e_)
            for fc in range(8):
                self.mm3(self.bank(pk), WH[:, 1024 * slot + 128 * fc:1024 * slot + 128 * (fc + 1)],
                         WL[:, 1024 * slot + 128 * fc:1024 * slot + 128 * (fc + 1)],
                         PH[:, 512 * fc:512 * (fc + 1)], PL[:, 512 * fc:512 * (fc + 1)],
                         bG + SEG0, self.pb(pk), start=(fc == 0), stop=(fc == 7))
                self.mm3(self.bank(pk + 1), WH[:, 2048 + 1024 * slot + 128 * fc:2048 + 1024 * slot + 128 * (fc + 1)],
                         WL[:, 2048 + 1024 * slot + 128 * fc:2048 + 1024 * slot + 128 * (fc + 1)],
                         PH[:, 512 * fc:512 * (fc + 1)], PL[:, 512 * fc:512 * (fc + 1)],
                         bU + SEG0, self.pb(pk + 1), start=(fc == 0), stop=(fc == 7))
            p.op("pe", lambda e, e_=e_, pk=pk: e.matmul(
                self.bank(pk + 2), self.ESEL[:, 128 * e_:128 * (e_ + 1)],
                self.WTP[0:32, 0:512],
                start=True, stop=False), r=[self.B("esel"), bWT], w=self.pb(pk + 2))
            p.op("pe", lambda e, e_=e_, pk=pk: e.matmul(
                self.bank(pk + 2), self.ESEL[:, 128 * e_:128 * (e_ + 1)],
                self.WTP[0:32, 512:1024],
                start=False, stop=True), r=[self.B("esel"), bWT], w=self.pb(pk + 2))
            xa, xb = self.fb(3 * slot + 0)
            ya, yb = self.fb(3 * slot + 1)
            p.op("act", lambda e, pk=pk, xa=xa: e.activation(out=xa[:, 0:512], in_=self.bank(pk),
                                                             func=AF.Silu), r=self.pb(pk), w=[xb])
            p.op("dve", lambda e, pk=pk, xa=xa: e.tensor_mul(xa[:, 512:1024], xa[:, 0:512],
                                                             self.bank(pk + 2)),
                 r=[xb] + self.pb(pk + 2), w=[xb])
            p.op("dve", lambda e, pk=pk, xa=xa, ya=ya: e.tensor_mul(ya[:, 0:512], xa[:, 512:1024],
                                                                    self.bank(pk + 1)),
                 r=[xb] + self.pb(pk + 1), w=[yb])
            hb = self.pseg(4096 + 512 * e_, 4096 + 512 * (e_ + 1))
            p.op("act", lambda e, e_=e_, ya=ya: e.copy(PH[:, 4096 + 512 * e_:4096 + 512 * (e_ + 1)],
                                                       ya[:, 0:512]), r=[yb], w=hb)
            p.op("dve", lambda e, e_=e_, ya=ya: e.tensor_sub(
                PL[:, 4096 + 512 * e_:4096 + 512 * (e_ + 1)], ya[:, 0:512],
                PH[:, 4096 + 512 * e_:4096 + 512 * (e_ + 1)]), r=[yb] + hb, w=hb)
        wbd = {0: self.wload("wd", 0, 0, 1024)}
        for e_ in range(32):
            slot = e_ % 2
            if e_ + 1 < 32:
                wbd[e_ + 1] = self.wload("wd", e_ + 1, 1024 * ((e_ + 1) % 2), 1024)
            bW = wbd.pop(e_)
            hb = self.pseg(4096 + 512 * e_, 4096 + 512 * (e_ + 1))
            for tt in range(TPC):
                hcol = 4096 + 512 * e_ + 128 * tt
                jobs = [(self.bank(2 * tt + sx),
                         WH[:, 1024 * slot + 512 * sx:1024 * slot + 512 * (sx + 1)],
                         WL[:, 1024 * slot + 512 * sx:1024 * slot + 512 * (sx + 1)],
                         e_ == 0, False, self.pb(2 * tt + sx)) for sx in range(2)]
                self.mmg(PH[:, hcol:hcol + 128], PL[:, hcol:hcol + 128], jobs, bW + hb)
        for sc in range(4):
            slot = 2 + sc % 2
            bW = self.wload("wsd", sc, 1024 * slot, 1024)
            hsb = self.wseg(4096 + 512 * sc, 4096 + 512 * (sc + 1))
            for tt in range(TPC):
                hcol = 4096 + 512 * sc + 128 * tt
                jobs = [(self.bank(2 * tt + sx),
                         WH[:, 1024 * slot + 512 * sx:1024 * slot + 512 * (sx + 1)],
                         WL[:, 1024 * slot + 512 * sx:1024 * slot + 512 * (sx + 1)],
                         False, sc == 3, self.pb(2 * tt + sx)) for sx in range(2)]
                self.mmg(WH[:, hcol:hcol + 128], WL[:, hcol:hcol + 128], jobs, bW + hsb)
        for tt in range(TPC):
            self.epilogue(self.bank(2 * tt, 2), self.pb(2 * tt, 2), b, TPC * ci + tt)

    def sload_only(self, b, ci):
        p = self.p
        for tt in range(TPC):
            t_abs = TPC * ci + tt
            bX = self.B(f"xch{tt}")
            p.op("sp", lambda e, t_abs=t_abs, tt=tt: e.dma_start(
                out=self.XCH[:, 1024 * tt:1024 * (tt + 1)],
                in_=self.x_in[b, 128 * t_abs:128 * (t_abs + 1), :]), w=[bX])

    def build(self, stages="A1234"):
        self.declare()
        self.alloc()
        self.p.alloc_sems()
        self.presplit()
        if "A" in stages:
            for b in range(BLOC):
                self.phase_a(b)
        for b in range(BLOC):
            for ci in range(NCHUNK):
                if "1" in stages:
                    self.s1_l0attn(b, ci)
                else:
                    self.sload_only(b, ci)
                if "2" in stages:
                    self.s2_ffn(b, ci)
                if "3" in stages:
                    self.s3_l1attn(b, ci)
                if "4" in stages:
                    self.s4_moe(b, ci)
                self.s5_store(b, ci)
        self.p.emit()
        self.close_ctx()
        return self.nc


DEBUG_STAGES = "A1234"
DEBUG_CORES = NCORE


def kernel(**inputs):
    g, shards = host_prep(inputs)
    shapes = {k: np.asarray(v).shape for k, v in g.items()}
    kb = KB(shapes)
    nc = kb.build(DEBUG_STAGES)
    in_maps = []
    for c in range(DEBUG_CORES):
        m = {"x": shards[c]}
        for name in PAIR_WEIGHTS + RAW_WEIGHTS + ["ident", "mask", "esel"]:
            m[name] = g[name]
        in_maps.append(m)
    res = run_bass_kernel_spmd(nc, in_maps, core_ids=list(range(DEBUG_CORES)))
    return np.concatenate([r["out"] for r in res.results], axis=0)


# version 09
# speedup vs baseline: 1.2132x; 1.2132x over previous
import numpy as np
import ml_dtypes
import concourse.bass as bass
import concourse.mybir as mybir
from concourse.bass_utils import run_bass_kernel_spmd

F32 = mybir.dt.float32
BF16 = mybir.dt.bfloat16
AF = mybir.ActivationFunctionType
ALU = mybir.AluOpType
AX = mybir.AxisListType

ENGINES = ("pe", "act", "dve", "sp", "gp")
DMA_ENGINES = ("sp", "gp")
GAP_MIN = 3

NCORE = 8
BLOC = 2
T = 2048
D = 1024
NT = 16
TC = 512
NCHUNK = 4
TPC = 4
H0, D0 = 16, 64
EPS_LN = 1e-5
EPS_ATT = 1e-6
NEG = -1e30


class Buf:
    __slots__ = ("name", "w", "r")

    def __init__(self, name="b"):
        self.name = name
        self.w = None
        self.r = []


class _Group:
    __slots__ = ("engine", "ops", "count", "closed")

    def __init__(self, engine):
        self.engine = engine
        self.ops = []
        self.count = None
        self.closed = False


class _Op:
    __slots__ = ("fn", "waits")

    def __init__(self, fn):
        self.fn = fn
        self.waits = {}


NSEM_DMA = 8


class Prog:

    def __init__(self, nc):
        self.nc = nc
        self.eng_groups = {e: [] for e in ENGINES}
        self.open = {e: None for e in ENGINES}
        self.count = {e: 0 for e in ENGINES}
        self.dma_seq = {e: 0 for e in DMA_ENGINES}
        self.dma_cnt = {}
        self.idx = {e: 0 for e in ENGINES}
        self.sems = {}
        self.nops = 0

    def alloc_sems(self):
        for e in ENGINES:
            if e in DMA_ENGINES:
                for j in range(NSEM_DMA):
                    self.sems[(e, j)] = self.nc.alloc_semaphore(name=f"sem_{e}{j}")
            else:
                self.sems[e] = self.nc.alloc_semaphore(name=f"sem_{e}")

    def _close(self, eng):
        g = self.open[eng]
        if g is None:
            return
        if eng in DMA_ENGINES:
            j = self.dma_seq[eng] % NSEM_DMA
            self.dma_seq[eng] += 1
            key = (eng, j)
            self.dma_cnt[key] = self.dma_cnt.get(key, 0) + 16 * len(g.ops)
            g.count = (key, self.dma_cnt[key])
        else:
            self.count[eng] += 1
            g.count = (eng, self.count[eng])
        g.closed = True
        self.open[eng] = None

    def fence(self, eng):
        self._close(eng)

    def op(self, eng, fn, r=(), w=(), serialize=False):
        o = _Op(fn)
        need = {}

        def add_need(key_cnt):
            key, cnt = key_cnt
            if need.get(key, 0) < cnt:
                need[key] = cnt

        myidx = self.idx[eng]
        hazard = eng in DMA_ENGINES or eng != "pe"
        for b in r:
            if b is None or b.w is None:
                continue
            we, wg, widx = b.w
            if we != eng:
                if not wg.closed:
                    self._close(we)
                add_need(wg.count)
            else:
                if eng in DMA_ENGINES or (hazard and (myidx - widx) <= GAP_MIN):
                    if not wg.closed:
                        self._close(we)
                    add_need(wg.count)
        for b in w:
            if b is None:
                continue
            if b.w is not None:
                we, wg, widx = b.w
                if we != eng or eng in DMA_ENGINES or (hazard and (myidx - widx) <= GAP_MIN):
                    if not wg.closed:
                        self._close(we)
                    add_need(wg.count)
            for re, rg, ridx in b.r:
                if re != eng or eng in DMA_ENGINES or (hazard and (myidx - ridx) <= GAP_MIN):
                    if not rg.closed:
                        self._close(re)
                    add_need(rg.count)
        if serialize and self.open[eng] is not None:
            self._close(eng)
            add_need((eng, self.count[eng]) if eng in DMA_ENGINES else (eng, self.count[eng]))
        if self.open[eng] is not None and any(
                (k == eng) or (isinstance(k, tuple) and k[0] == eng) for k in need):
            self._close(eng)
        o.waits = need
        g = self.open[eng]
        if g is None:
            g = _Group(eng)
            self.open[eng] = g
            self.eng_groups[eng].append(g)
        g.ops.append(o)
        self.idx[eng] += 1
        self.nops += 1
        for b in r:
            if b is not None:
                b.r.append((eng, g, myidx))
        for b in w:
            if b is not None:
                b.w = (eng, g, myidx)
                b.r = []
        return o

    def emit(self, final_engine="gp"):
        nc = self.nc
        for e in ENGINES:
            self._close(e)
        finals = {}
        for e in ENGINES:
            if e in DMA_ENGINES:
                for j in range(NSEM_DMA):
                    if self.dma_cnt.get((e, j), 0) > 0:
                        finals[(e, j)] = self.dma_cnt[(e, j)]
            elif self.count[e] > 0:
                finals[e] = self.count[e]
        eng_obj = {"pe": nc.tensor, "act": nc.scalar, "dve": nc.vector,
                   "sp": nc.sync, "gp": nc.gpsimd}

        def make(engine):
            def body(eng):
                hwm = {}

                def wait(key, val):
                    if hwm.get(key, 0) < val:
                        eng.wait_ge(self.sems[key], val)
                        hwm[key] = val

                for g in self.eng_groups[engine]:
                    n = len(g.ops)
                    if engine in DMA_ENGINES:
                        key, cnt = g.count
                        prev = cnt - 16 * n
                        if prev > 0:
                            wait(key, prev)
                    for i, o in enumerate(g.ops):
                        for se in sorted(o.waits, key=str):
                            wait(se, o.waits[se])
                        ins = o.fn(eng_obj[engine])
                        if engine in DMA_ENGINES:
                            ins.then_inc(self.sems[g.count[0]], 16)
                        elif i == n - 1:
                            ins.then_inc(self.sems[engine], 1)
                if engine == final_engine:
                    for se in sorted(finals, key=str):
                        wait(se, finals[se])
            return body

        with nc.Block() as block:
            block.tensor(make("pe"))
            block.scalar(make("act"))
            block.vector(make("dve"))
            block.sync(make("sp"))
            block.gpsimd(make("gp"))


def _lhsT_blocks(w):
    Fdim, Odim = w.shape
    nf, no = Fdim // 128, Odim // 128
    return np.ascontiguousarray(
        w.reshape(nf, 128, no, 128).transpose(2, 1, 0, 3).reshape(no, 128, nf * 128))


def _rhs_blocks(w):
    Fdim, Odim = w.shape
    return np.ascontiguousarray(w.reshape(Fdim // 128, 128, Odim))


def host_prep(inputs):
    f = np.float32
    g = {}
    WqkvT = np.ascontiguousarray(np.asarray(inputs["L0_attn_Wqkv"], f).T)
    g["wq0"] = _lhsT_blocks(WqkvT[:, 0:D])
    g["wkv0"] = _rhs_blocks(WqkvT[:, D:3 * D])
    g["wpr0"] = _rhs_blocks(np.ascontiguousarray(np.asarray(inputs["L0_attn_Wproj"], f).T))
    g["bpr0"] = np.asarray(inputs["L0_attn_bproj"], f).reshape(1, 1, D)
    g["w1"] = _lhsT_blocks(np.asarray(inputs["L0_ffn_W1"], f))
    g["b1"] = np.ascontiguousarray(np.asarray(inputs["L0_ffn_b1"], f).reshape(32, 128).T).reshape(1, 128, 32)
    g["w2"] = _rhs_blocks(np.asarray(inputs["L0_ffn_W2"], f))
    g["b2"] = np.asarray(inputs["L0_ffn_b2"], f).reshape(1, 1, D)
    for nm, key in (("wq1", "L1_attn_Wq"), ("wk1", "L1_attn_Wk"), ("wv1", "L1_attn_Wv")):
        w = np.asarray(inputs[key], f)
        g[nm] = _rhs_blocks(np.ascontiguousarray(w.transpose(1, 0, 2).reshape(D, D)))
    g["wpr1"] = _rhs_blocks(np.ascontiguousarray(np.asarray(inputs["L1_attn_proj_w"], f).T))
    g["bpr1"] = np.asarray(inputs["L1_attn_proj_b"], f).reshape(1, 1, D)
    g["wrt"] = _rhs_blocks(np.ascontiguousarray(np.asarray(inputs["L1_ffn_router_w"], f).T))
    g["brt"] = np.asarray(inputs["L1_ffn_router_bias"], f).reshape(1, 1, 32)
    gw = np.asarray(inputs["L1_ffn_gate_w"], f).transpose(0, 2, 1)
    uw = np.asarray(inputs["L1_ffn_up_w"], f).transpose(0, 2, 1)
    g["wg"] = np.concatenate([_lhsT_blocks(np.ascontiguousarray(gw[e])) for e in range(32)], 0)
    g["wu"] = np.concatenate([_lhsT_blocks(np.ascontiguousarray(uw[e])) for e in range(32)], 0)
    dw = np.asarray(inputs["L1_ffn_down_w"], f).transpose(0, 2, 1)
    g["wd"] = np.ascontiguousarray(dw)
    g["wsg"] = _lhsT_blocks(np.ascontiguousarray(np.asarray(inputs["L1_ffn_sgate_w"], f).T))
    g["wsu"] = _lhsT_blocks(np.ascontiguousarray(np.asarray(inputs["L1_ffn_sup_w"], f).T))
    g["wsd"] = _rhs_blocks(np.ascontiguousarray(np.asarray(inputs["L1_ffn_sdown_w"], f).T))
    g["ident"] = np.eye(128, dtype=f)
    g["esel"] = np.kron(np.eye(32, dtype=f), np.ones((1, 128), f))
    r = np.arange(128)
    g["mask"] = np.where(r[None, :] <= r[:, None], 0.0, NEG).astype(f)
    x = np.asarray(inputs["L0_attn_x"], f)
    shards = [np.ascontiguousarray(x[2 * c:2 * c + 2]) for c in range(NCORE)]
    return g, shards


PAIR_WEIGHTS = ["wq0", "wkv0", "wpr0", "bpr0", "w1", "w2", "b2", "wq1", "wk1",
                "wv1", "wpr1", "bpr1", "wrt", "brt", "wg", "wu", "wd", "wsg",
                "wsu", "wsd"]
RAW_WEIGHTS = ["b1"]


class KB:

    def __init__(self, shapes):
        self.nc = bass.Bass("TRN2", target_bir_lowering=False, debug=False)
        self.p = Prog(self.nc)
        self.shapes = shapes
        self.bufs = {}
        self.dram = {}
        self.pair = {}
        self._ctx = []

    def B(self, name):
        return self.bufs.setdefault(name, Buf(name))

    def declare(self):
        nc = self.nc
        self.x_in = nc.dram_tensor("x", [BLOC, T, D], F32, kind="ExternalInput").ap()
        self.out = nc.dram_tensor("out", [BLOC, T, D], F32, kind="ExternalOutput").ap()
        for name in PAIR_WEIGHTS + RAW_WEIGHTS + ["ident", "mask", "esel"]:
            sh = list(self.shapes[name])
            self.dram[name] = nc.dram_tensor(name, sh, F32, kind="ExternalInput").ap()
        for name in PAIR_WEIGHTS:
            nch, P, C = self.shapes[name]
            self.pair[name] = nc.dram_tensor(
                name + "_pr", [nch, P, 2 * C], BF16, kind="Internal").ap()

    def alloc(self):
        nc = self.nc
        cm = []

        def sb(name, shape, dt):
            g = nc.sbuf_tensor(name, shape, dt)
            t = g.__enter__()
            self._ctx.append(g)
            return t

        self.PH = sb("PH", [128, 20480], BF16)
        self.PL = sb("PL", [128, 20480], BF16)
        self.XCH = sb("XCH", [128, 4096], F32)
        self.FA = sb("FA", [128, 6144], F32)
        self.FB = sb("FB", [128, 8192], F32)
        self.WX = sb("WX", [128, 12288], BF16)
        self.WH = self.WX[:, 0:6144]
        self.WL = self.WX[:, 6144:12288]
        self.KVS = sb("KVS", [128, 4160], BF16)
        self.STAT = sb("STAT", [128, 96], F32)
        self.SM = sb("SM", [128, 64], F32)
        self.IDN = sb("IDN", [128, 128], F32)
        self.B1S = sb("B1S", [128, 32], F32)
        self.MSK = sb("MSK", [128, 128], F32)
        self.ONE = sb("ONE", [1, 128], BF16)
        self.ZR = sb("ZR", [1, 512], BF16)
        self.CST = sb("CST", [128, 4], F32)
        self.WTP = sb("WTP", [128, 1024], BF16)
        self.ESEL = sb("ESEL", [32, 4096], BF16)
        pg = nc.psum_tensor("PS", [128, 4096], F32)
        self.PS = pg.__enter__()
        self._ctx.append(pg)

    def close_ctx(self):
        for g in reversed(self._ctx):
            g.__exit__(None, None, None)

    def bank(self, i, n=1):
        return self.PS[:, 512 * i:512 * (i + 1 + (n - 1))]

    def pb(self, i, n=1):
        return [self.B(f"psbank{k}") for k in range(i, i + n)]

    def pseg(self, c0, c1):
        return [self.B(f"PH{k}") for k in range(c0 // 1024, (c1 + 1023) // 1024)]

    def fb(self, idx):
        return self.FB[:, 1024 * idx:1024 * (idx + 1)], self.B(f"FB{idx}")

    def keepalive(self, bank, n):
        wb = self.pb(bank)
        for i in range(n):
            self.p.op("pe", lambda e: e.matmul(
                self.bank(bank), self.ONE[:], self.ZR[:],
                start=True, stop=True), r=[self.B("one"), self.B("zr")], w=wb)

    def wseg(self, c0, c1):
        return [self.B(f"WH{k}") for k in range(c0 // 1024, (c1 + 1023) // 1024)]

    def smseg(self, c0, c1):
        return [self.B(f"SM{k}") for k in range(c0 // 16, (c1 + 15) // 16)]

    def faseg(self, c0, c1):
        return [self.B(f"FA{k}") for k in range(c0 // 1024, (c1 + 1023) // 1024)]

    def mm3(self, psum_ap, lh, ll, rh, rl, rbufs, wbufs, start, stop, tp=None,
            serialize=False):
        p = self.p
        p.op("pe", lambda e: e.matmul(psum_ap, lh, rh, start=start, stop=False,
                                      tile_position=tp), r=rbufs, w=wbufs,
             serialize=serialize)
        p.op("pe", lambda e: e.matmul(psum_ap, lh, rl, start=False, stop=False,
                                      tile_position=tp), r=rbufs, w=wbufs)
        p.op("pe", lambda e: e.matmul(psum_ap, ll, rh, start=False, stop=stop,
                                      tile_position=tp), r=rbufs, w=wbufs)

    def mmg(self, lh, ll, jobs, rbufs, tp=None):
        p = self.p
        for ps_ap, rh, rl, st, sp_, wb in jobs:
            p.op("pe", lambda e, ps_ap=ps_ap, rh=rh, st=st: e.matmul(
                ps_ap, lh, rh, start=st, stop=False, tile_position=tp), r=rbufs, w=wb)
            p.op("pe", lambda e, ps_ap=ps_ap, rl=rl: e.matmul(
                ps_ap, lh, rl, start=False, stop=False, tile_position=tp), r=rbufs, w=wb)
        for ps_ap, rh, rl, st, sp_, wb in jobs:
            p.op("pe", lambda e, ps_ap=ps_ap, rh=rh, sp_=sp_: e.matmul(
                ps_ap, ll, rh, start=False, stop=sp_, tile_position=tp), r=rbufs, w=wb)

    def wload(self, name, ch, whc, n, prows=128):
        pr = self.pair[name]
        C = self.shapes[name][2]
        p = self.p
        bufs = self.wseg(whc, whc + n)
        dst = self.WX[:prows].rearrange("p (h c) -> p h c", c=6144)[:, :, whc:whc + n]
        src = pr[ch].rearrange("p (h c) -> p h c", c=C)[:prows, :, 0:n]
        p.op("sp", lambda e, dst=dst, src=src: e.dma_start(out=dst, in_=src),
             r=[self.B(name + "_pair")], w=bufs)
        return bufs

    def presplit(self):
        p, nc = self.p, self.nc
        p.op("sp", lambda e: e.dma_start(out=self.IDN[:], in_=self.dram["ident"][:]),
             w=[self.B("ident")])
        p.op("sp", lambda e: e.dma_start(out=self.MSK[:], in_=self.dram["mask"][:]),
             w=[self.B("mask")])
        p.op("dve", lambda e: e.memset(self.ONE[:], 1.0), w=[self.B("one")])
        p.op("dve", lambda e: e.memset(self.ZR[:], 0.0), w=[self.B("zr")])
        p.op("dve", lambda e: e.memset(self.CST[:, 0:1], 0.125), w=[self.B("cst")])
        p.op("sp", lambda e: e.dma_start(out=self.B1S[:], in_=self.dram["b1"][0]),
             w=[self.B("b1s")])
        p.op("sp", lambda e: e.dma_start(out=self.FA[0:32, 0:4096], in_=self.dram["esel"][:]),
             w=[self.B("esel32")])
        p.op("act", lambda e: e.copy(self.ESEL[:], self.FA[0:32, 0:4096]),
             r=[self.B("esel32")], w=[self.B("esel")])
        pc = 0
        for name in PAIR_WEIGHTS:
            nch, P, C = self.shapes[name]
            pr = self.pair[name]
            for ch in range(nch):
                for c0 in range(0, C, 1024):
                    cw = min(1024, C - c0)
                    slot = pc % 2
                    pc += 1
                    fb = self.FB[:P, 1024 * slot:1024 * slot + cw]
                    wh = self.WH[:P, 1024 * slot:1024 * slot + cw]
                    wl = self.WL[:P, 1024 * slot:1024 * slot + cw]
                    bF = self.B(f"FB{slot}")
                    bW = self.wseg(1024 * slot, 1024 * slot + cw)
                    src = self.dram[name]
                    p.op("sp", lambda e, src=src, ch=ch, c0=c0, cw=cw, fb=fb:
                         e.dma_start(out=fb, in_=src[ch, :, c0:c0 + cw]), w=[bF])
                    p.op("act", lambda e, fb=fb, wh=wh: e.copy(wh, fb), r=[bF], w=bW)
                    p.op("dve", lambda e, fb=fb, wh=wh, wl=wl: e.tensor_sub(wl, fb, wh),
                         r=[bF] + bW, w=bW)
                    p.op("gp", lambda e, pr=pr, ch=ch, c0=c0, cw=cw, wh=wh:
                         e.dma_start(out=pr[ch, :, c0:c0 + cw], in_=wh), r=bW,
                         w=[self.B(name + "_pair")])
                    p.op("gp", lambda e, pr=pr, ch=ch, c0=c0, cw=cw, wl=wl, C=C:
                         e.dma_start(out=pr[ch, :, C + c0:C + c0 + cw], in_=wl), r=bW,
                         w=[self.B(name + "_pair")])

    def ln_stats_apply(self, src_ap, src_buf, stat_col, compute_stats, out32, out32_buf,
                       dump_ap=None, dump_buf=None):
        p = self.p
        S = self.STAT
        nm = S[:, stat_col:stat_col + 1]
        rs = S[:, stat_col + 1:stat_col + 2]
        bS = self.B(f"stat{stat_col}")
        if compute_stats:
            sm = self.SM
            SM0 = self.B("SM0")
            p.op("dve", lambda e: e.reduce_sum(sm[:, 0:1], src_ap, axis=AX.X),
                 r=[src_buf], w=[SM0])
            p.op("dve", lambda e: e.tensor_scalar(nm, sm[:, 0:1], -1.0 / D, None, ALU.mult),
                 r=[SM0], w=[bS])
            p.op("act", lambda e: e.activation(out=dump_ap, in_=src_ap,
                                               func=AF.Square, bias=nm,
                                               accum_out=sm[:, 1:2]),
                 r=[src_buf, bS], w=[dump_buf, SM0])
            p.op("dve", lambda e: e.tensor_scalar(sm[:, 2:3], sm[:, 1:2], 1.0 / D,
                                                  EPS_LN, ALU.mult, ALU.add),
                 r=[SM0], w=[SM0])
            p.op("act", lambda e: e.activation(out=sm[:, 3:4], in_=sm[:, 2:3], func=AF.Sqrt),
                 r=[SM0], w=[SM0])
            p.op("dve", lambda e: e.reciprocal(rs, sm[:, 3:4]), r=[SM0], w=[bS])
        p.op("dve", lambda e: e.tensor_scalar(out32, src_ap, nm, rs, ALU.add, ALU.mult),
             r=[src_buf, bS], w=[out32_buf])

    def transpose_split(self, src32, src_buf, ph_col, nch=8, pbank=5,
                        chunk_stride=128):
        p = self.p
        extent = ph_col + (chunk_stride * (nch - 1) + 128)
        pair_bufs = self.pseg(ph_col, extent)
        for half in range(0, nch, 4):
            hn = min(4, nch - half)
            pb = self.pb(pbank)[0]
            for c in range(hn):
                cc = half + c
                p.op("pe", lambda e, cc=cc, c=c: e.transpose(
                    self.PS[:, 512 * pbank + 128 * c:512 * pbank + 128 * c + 128],
                    src32[:, 128 * cc:128 * cc + 128], self.IDN[:]),
                    r=[src_buf, self.B("ident")], w=[pb])
            ps = self.PS[:, 512 * pbank:512 * pbank + 128 * hn]
            if chunk_stride == 128:
                dst_h = self.PH[:, ph_col + 128 * half:ph_col + 128 * (half + hn)]
                dst_l = self.PL[:, ph_col + 128 * half:ph_col + 128 * (half + hn)]
            else:
                c0 = ph_col + chunk_stride * half
                span = chunk_stride * hn
                dst_h = self.PH[:, c0:c0 + span].rearrange(
                    "p (c k) -> p c k", k=chunk_stride)[:, :, 0:128]
                dst_l = self.PL[:, c0:c0 + span].rearrange(
                    "p (c k) -> p c k", k=chunk_stride)[:, :, 0:128]
                ps = ps.rearrange("p (c k) -> p c k", k=128)
            p.op("act", lambda e, ps=ps, d=dst_h: e.copy(d, ps), r=[pb], w=pair_bufs)
            p.op("dve", lambda e, ps=ps, d=dst_h, dl=dst_l: e.tensor_sub(dl, ps, d),
                 r=[pb] + pair_bufs, w=pair_bufs)

    def phase_a(self, b):
        p = self.p
        PH, PL, FB, PS, WH, WL = self.PH, self.PL, self.FB, self.PS, self.WH, self.WL
        pr = self.pair["wkv0"]
        bW = self.pseg(4096, 20480)
        for ch in range(8):
            p.op("sp", lambda e, ch=ch, pr=pr: e.dma_start(
                out=PH[:, 4096 + 2048 * ch:4096 + 2048 * (ch + 1)], in_=pr[ch, :, 0:2048]),
                r=[self.B("wkv0_pair")], w=self.pseg(4096 + 2048 * ch, 4096 + 2048 * (ch + 1)))
            p.op("sp", lambda e, ch=ch, pr=pr: e.dma_start(
                out=PL[:, 4096 + 2048 * ch:4096 + 2048 * (ch + 1)], in_=pr[ch, :, 2048:4096]),
                r=[self.B("wkv0_pair")], w=self.pseg(4096 + 2048 * ch, 4096 + 2048 * (ch + 1)))
        for pt in range(2):
            vb = self.wseg(2048 + 2048 * pt, 2048 + 2048 * pt + 1040)
            p.op("dve", lambda e, pt=pt: e.memset(
                WH[:, 2048 + 2048 * pt + 64:2048 + 2048 * pt + 64 + 65 * 15 + 1:65], 1.0), w=vb)
            p.op("dve", lambda e, pt=pt: e.memset(
                WL[:, 2048 + 2048 * pt + 64:2048 + 2048 * pt + 64 + 65 * 15 + 1:65], 0.0), w=vb)
        for bk in (6, 7):
            p.op("pe", lambda e, bk=bk: e.matmul(
                self.bank(bk), self.ZR[:1, 0:128], self.ZR[:],
                start=True, stop=False), r=[self.B("zr")], w=self.pb(bk))
        for t in range(NT):
            pt = t % 2
            xa, xb = self.fb(3 * pt + 0)
            ya, yb = self.fb(3 * pt + 1)
            da, db = self.fb(3 * pt + 2)
            xtb = self.pseg(1024 * pt, 1024 * (pt + 1))
            kb = self.wseg(1024 * pt, 1024 * (pt + 1))
            vb = self.wseg(2048 + 2048 * pt, 2048 + 2048 * pt + 1040)
            p.op("sp", lambda e, t=t, xa=xa: e.dma_start(
                out=xa, in_=self.x_in[b, 128 * t:128 * (t + 1), :]), w=[xb])
            self.ln_stats_apply(xa, xb, 2 * (NT * b + t), True, ya, yb,
                                dump_ap=da, dump_buf=db)
            self.transpose_split(ya, yb, 1024 * pt, chunk_stride=128)
            for fc in range(8):
                wh = PH[:, 4096 + 2048 * fc:4096 + 2048 * (fc + 1)]
                wl = PL[:, 4096 + 2048 * fc:4096 + 2048 * (fc + 1)]
                xh = PH[:, 1024 * pt + 128 * fc:1024 * pt + 128 * (fc + 1)]
                xl = PL[:, 1024 * pt + 128 * fc:1024 * pt + 128 * (fc + 1)]
                jobs = []
                for sx in range(4):
                    jobs.append((self.bank(sx), wh[:, 512 * sx:512 * (sx + 1)],
                                 wl[:, 512 * sx:512 * (sx + 1)], fc == 0, fc == 7,
                                 self.pb(sx)))
                self.mmg(xh, xl, jobs, bW + xtb)
            p.op("dve", lambda e, da=da: e.tensor_scalar(da, self.bank(0, 2), 0.0,
                                                         None, ALU.min),
                 r=self.pb(0, 2), w=[db])
            p.op("act", lambda e, da=da, xa=xa: e.activation(out=xa, in_=da, func=AF.Exp),
                 r=[db], w=[xb])
            p.op("dve", lambda e, ya=ya, xa=xa: e.scalar_tensor_tensor(
                ya, self.bank(0, 2), 0.0, xa, ALU.max, ALU.add),
                 r=self.pb(0, 2) + [xb], w=[yb])
            p.op("act", lambda e, pt=pt, ya=ya: e.copy(WH[:, 1024 * pt:1024 * (pt + 1)], ya),
                 r=[yb], w=kb)
            p.op("dve", lambda e, pt=pt, ya=ya: e.tensor_sub(
                WL[:, 1024 * pt:1024 * (pt + 1)], ya, WH[:, 1024 * pt:1024 * (pt + 1)]),
                 r=[yb] + kb, w=kb)
            vo = 2048 + 2048 * pt
            vdst_h = WH[:, vo:vo + 65 * 16].rearrange("p (h c) -> p h c", c=65)[:, :, 0:64]
            vdst_l = WL[:, vo:vo + 65 * 16].rearrange("p (h c) -> p h c", c=65)[:, :, 0:64]
            vsrc = self.bank(2, 2).rearrange("p (h c) -> p h c", c=64)
            p.op("act", lambda e, vdst_h=vdst_h, vsrc=vsrc: e.copy(vdst_h, vsrc),
                 r=self.pb(2, 2), w=vb)
            p.op("dve", lambda e, vdst_l=vdst_l, vsrc=vsrc, vdst_h=vdst_h:
                 e.tensor_tensor(vdst_l, vsrc, vdst_h, ALU.subtract),
                 r=self.pb(2, 2) + vb, w=vb)
            for h in range(H0):
                pr, rowo = h // 2, (h % 2) * 64
                if pr < 7:
                    kvap = PS[rowo:rowo + 64, 3072 + 65 * pr:3072 + 65 * pr + 65]
                else:
                    kvap = PS[rowo:rowo + 64, 3584:3584 + 65]
                kh = WH[:, 1024 * pt + 64 * h:1024 * pt + 64 * (h + 1)]
                kl = WL[:, 1024 * pt + 64 * h:1024 * pt + 64 * (h + 1)]
                vh = WH[:, vo + 65 * h:vo + 65 * h + 65]
                vl = WL[:, vo + 65 * h:vo + 65 * h + 65]
                tp = (0, rowo) if rowo else None
                e_flag = (t == NT - 1) and (h in (12, 13, 14, 15))
                wb2 = self.pb(6 if pr < 7 else 7)
                p.op("pe", lambda e, kvap=kvap, kh=kh, vh=vh, tp=tp:
                     e.matmul(kvap, kh, vh, start=False, stop=False, tile_position=tp),
                     r=kb + vb, w=wb2, serialize=True)
                p.op("pe", lambda e, kvap=kvap, kh=kh, vl=vl, tp=tp:
                     e.matmul(kvap, kh, vl, start=False, stop=False, tile_position=tp),
                     r=kb + vb, w=wb2)
                p.op("pe", lambda e, kvap=kvap, kl=kl, vh=vh, tp=tp, st=e_flag:
                     e.matmul(kvap, kl, vh, start=False, stop=st, tile_position=tp),
                     r=kb + vb, w=wb2)
        bKVp = self.B(f"kvpair{b}")
        src_a = PS[:, 3072:3072 + 455]
        src_b = PS[:, 3584:3584 + 65]
        ka = b * 2080
        p.op("act", lambda e: e.copy(self.KVS[:, ka:ka + 455], src_a), r=self.pb(6), w=[bKVp])
        p.op("act", lambda e: e.copy(self.KVS[:, ka + 455:ka + 520], src_b), r=self.pb(7), w=[bKVp])
        p.op("dve", lambda e: e.tensor_sub(self.KVS[:, ka + 1040:ka + 1495], src_a,
                                           self.KVS[:, ka:ka + 455]), r=self.pb(6) + [bKVp], w=[bKVp])
        p.op("dve", lambda e: e.tensor_sub(self.KVS[:, ka + 1495:ka + 1560], src_b,
                                           self.KVS[:, ka + 455:ka + 520]), r=self.pb(7) + [bKVp], w=[bKVp])

    def epilogue(self, ps_ap, ps_bufs, b, tt_abs, smc=48):
        p, sm, FB = self.p, self.SM, self.FB
        xt = self.XCH[:, 1024 * (tt_abs % TPC):1024 * (tt_abs % TPC) + 1024]
        bX = self.B(f"xch{tt_abs % TPC}")
        FB1, FB2 = self.B("FB1"), self.B("FB2")
        c = smc
        SM3 = self.smseg(48, 64)
        p.op("dve", lambda e: e.reduce_sum(sm[:, c:c + 1], ps_ap, axis=AX.X),
             r=ps_bufs, w=SM3)
        p.op("dve", lambda e: e.tensor_scalar(sm[:, c + 1:c + 2], sm[:, c:c + 1],
                                              -1.0 / D, None, ALU.mult),
             r=SM3, w=SM3)
        p.op("act", lambda e: e.activation(out=FB[:, 2048:3072], in_=ps_ap, func=AF.Square,
                                           bias=sm[:, c + 1:c + 2], accum_out=sm[:, c + 2:c + 3]),
             r=ps_bufs + SM3, w=[FB2] + SM3)
        p.op("dve", lambda e: e.tensor_scalar(sm[:, c + 3:c + 4], sm[:, c + 2:c + 3],
                                              1.0 / D, EPS_LN, ALU.mult, ALU.add),
             r=SM3, w=SM3)
        p.op("act", lambda e: e.activation(out=sm[:, c + 4:c + 5], in_=sm[:, c + 3:c + 4],
                                           func=AF.Sqrt), r=SM3, w=SM3)
        p.op("dve", lambda e: e.reciprocal(sm[:, c + 5:c + 6], sm[:, c + 4:c + 5]),
             r=SM3, w=SM3)
        p.op("dve", lambda e: e.tensor_scalar(FB[:, 1024:2048], ps_ap, sm[:, c + 1:c + 2],
                                              sm[:, c + 5:c + 6], ALU.add, ALU.mult),
             r=ps_bufs + SM3, w=[FB1])
        p.op("dve", lambda e: e.tensor_add(xt, xt, FB[:, 1024:2048]), r=[FB1, bX], w=[bX])

    def s1_l0attn(self, b, ci):
        p = self.p
        PH, PL, FB, FA, PS, WH, WL = self.PH, self.PL, self.FB, self.FA, self.PS, self.WH, self.WL
        self.keepalive(0, 48)
        bXT = self.pseg(0, 4096)
        for tt in range(TPC):
            t_abs = TPC * ci + tt
            bX = self.B(f"xch{tt}")
            pt = tt % 2
            ya, yb = self.fb(3 * pt + 1)
            p.op("sp", lambda e, t_abs=t_abs, tt=tt: e.dma_start(
                out=self.XCH[:, 1024 * tt:1024 * (tt + 1)],
                in_=self.x_in[b, 128 * t_abs:128 * (t_abs + 1), :]), w=[bX])
            self.ln_stats_apply(self.XCH[:, 1024 * tt:1024 * (tt + 1)], bX,
                                2 * (NT * b + t_abs), False, ya, yb)
            self.transpose_split(ya, yb, 128 * tt, chunk_stride=512)
        wbq = {0: self.wload("wq0", 0, 0, 1024)}
        for oc in range(8):
            slot = oc % 2
            if oc + 1 < 8:
                wbq[oc + 1] = self.wload("wq0", oc + 1, 1024 * ((oc + 1) % 2), 1024)
            bWq = wbq.pop(oc)
            for fc in range(8):
                self.mm3(self.bank(slot), WH[:, 1024 * slot + 128 * fc:1024 * slot + 128 * (fc + 1)],
                         WL[:, 1024 * slot + 128 * fc:1024 * slot + 128 * (fc + 1)],
                         PH[:, 512 * fc:512 * (fc + 1)], PL[:, 512 * fc:512 * (fc + 1)],
                         bWq + bXT, self.pb(slot), start=(fc == 0), stop=(fc == 7))
            xa, xb = self.fb(3 * slot + 0)
            ya, yb = self.fb(3 * slot + 1)
            da, db = self.fb(3 * slot + 2)
            p.op("dve", lambda e, slot=slot, da=da: e.tensor_scalar(
                da[:, 0:512], self.bank(slot), 0.0, None, ALU.min),
                r=self.pb(slot), w=[db])
            p.op("act", lambda e, da=da, xa=xa: e.activation(
                out=xa[:, 0:512], in_=da[:, 0:512], func=AF.Exp), r=[db], w=[xb])
            p.op("dve", lambda e, slot=slot, ya=ya, xa=xa: e.scalar_tensor_tensor(
                ya[:, 0:512], self.bank(slot), 0.0, xa[:, 0:512], ALU.max, ALU.add),
                r=self.pb(slot) + [xb], w=[yb])
            qh = PH[:, 4096 + 512 * oc:4096 + 512 * (oc + 1)]
            ql = PL[:, 4096 + 512 * oc:4096 + 512 * (oc + 1)]
            qsg = self.pseg(4096 + 512 * oc, 4096 + 512 * (oc + 1))
            p.op("act", lambda e, qh=qh, ya=ya: e.copy(qh, ya[:, 0:512]), r=[yb], w=qsg)
            p.op("dve", lambda e, qh=qh, ql=ql, ya=ya: e.tensor_sub(
                ql, ya[:, 0:512], qh), r=[yb] + qsg, w=qsg)
        bQ = self.pseg(4096, 8192)
        ka = b * 2080
        for tt in range(TPC):
            base = 2 if tt % 2 == 0 else 5
            for h in range(H0):
                rowo = (h % 2) * 64
                pcol = 512 * base + 512 * (h // 7) + 65 * (h % 7)
                kvcol = ka + (65 * (h // 2) if h // 2 < 7 else 455)
                qcol = 4096 + 512 * (h // 2) + 128 * tt
                lh = PH[rowo:rowo + 64, qcol:qcol + 128]
                ll = PL[rowo:rowo + 64, qcol:qcol + 128]
                rh = self.KVS[rowo:rowo + 64, kvcol:kvcol + 65]
                rl = self.KVS[rowo:rowo + 64, 1040 + kvcol:1040 + kvcol + 65]
                self.mm3(PS[:, pcol:pcol + 65], lh, ll, rh, rl,
                         bQ + [self.B(f"kvpair{b}")], self.pb(base + h // 7), start=True,
                         stop=True, tp=(rowo, 0) if rowo else None, serialize=True)
            for a in range(3):
                nh = 7 if a < 2 else 2
                c0 = 512 * base + 512 * a + 64
                src = PS[:, c0:c0 + 65 * (nh - 1) + 1:65]
                p.op("dve", lambda e, a=a, src=src, nh=nh: e.tensor_copy(
                    self.SM[:, 7 * a:7 * a + nh], src), r=self.pb(base + a), w=self.smseg(0, 16))
            p.op("dve", lambda e: e.tensor_scalar(self.SM[:, 16:32], self.SM[:, 0:16],
                                                  EPS_ATT, None, ALU.add),
                 r=self.smseg(0, 16), w=self.smseg(16, 32))
            p.op("dve", lambda e: e.reciprocal(self.SM[:, 32:48], self.SM[:, 16:32]),
                 r=self.smseg(16, 32), w=self.smseg(32, 48))
            fao = 1024 * (tt % 2)
            bAO = self.faseg(fao, fao + 1024)[0]
            for h in range(H0):
                pcol = 512 * base + 512 * (h // 7) + 65 * (h % 7)
                p.op("dve", lambda e, h=h, pcol=pcol, fao=fao: e.tensor_scalar(
                    FA[:, fao + 64 * h:fao + 64 * (h + 1)], PS[:, pcol:pcol + 64],
                    self.SM[:, 32 + h:33 + h], None, ALU.mult),
                    r=self.pb(base + h // 7) + self.smseg(32, 48), w=[bAO])
            self.transpose_split(FA[:, fao:fao + 1024], bAO, 8192 + 1024 * tt,
                                 chunk_stride=128, pbank=tt % 2)
        bB = self.wload("bpr0", 0, 4096, 1024, prows=1)
        for tt in range(TPC):
            pbi = 2 if tt % 2 == 0 else 5
            bSrc = self.pseg(8192 + 1024 * tt, 8192 + 1024 * (tt + 1))
            for fc in range(8):
                slot = fc % 2
                bWp = self.wload("wpr0", fc, 2048 + 1024 * slot, 1024)
                jobs = [(self.bank(pbi + sx),
                         WH[:, 2048 + 1024 * slot + 512 * sx:2048 + 1024 * slot + 512 * (sx + 1)],
                         WL[:, 2048 + 1024 * slot + 512 * sx:2048 + 1024 * slot + 512 * (sx + 1)],
                         fc == 0, False, self.pb(pbi + sx)) for sx in range(2)]
                self.mmg(PH[:, 8192 + 1024 * tt + 128 * fc:8192 + 1024 * tt + 128 * (fc + 1)],
                         PL[:, 8192 + 1024 * tt + 128 * fc:8192 + 1024 * tt + 128 * (fc + 1)],
                         jobs, bWp + bSrc)
            for sx in range(2):
                p.op("pe", lambda e, pbi=pbi, sx=sx: e.matmul(
                    self.bank(pbi + sx), self.ONE[:], WH[:1, 4096 + 512 * sx:4096 + 512 * (sx + 1)],
                    start=False, stop=False), r=[self.B("one")] + bB, w=self.pb(pbi + sx))
                p.op("pe", lambda e, pbi=pbi, sx=sx: e.matmul(
                    self.bank(pbi + sx), self.ONE[:], WL[:1, 4096 + 512 * sx:4096 + 512 * (sx + 1)],
                    start=False, stop=True), r=[self.B("one")] + bB, w=self.pb(pbi + sx))
            self.epilogue(self.bank(pbi, 2), self.pb(pbi, 2), b, TPC * ci + tt)

    def s5_store(self, b, ci):
        p = self.p
        for tt in range(TPC):
            t_abs = TPC * ci + tt
            bX = self.B(f"xch{tt}")
            p.op("gp", lambda e, t_abs=t_abs, tt=tt: e.dma_start(
                out=self.out[b, 128 * t_abs:128 * (t_abs + 1), :],
                in_=self.XCH[:, 1024 * tt:1024 * (tt + 1)]), r=[bX], w=[self.B("outd")])

    def s2_ffn(self, b, ci):
        p = self.p
        PH, PL, FB, WH, WL = self.PH, self.PL, self.FB, self.WH, self.WL
        self.keepalive(0, 48)
        bXT = self.pseg(0, 4096)
        for tt in range(TPC):
            bX = self.B(f"xch{tt}")
            pt = tt % 2
            ya, yb = self.fb(3 * pt + 1)
            da, db = self.fb(3 * pt + 2)
            self.ln_stats_apply(self.XCH[:, 1024 * tt:1024 * (tt + 1)], bX,
                                64, True, ya, yb, dump_ap=da, dump_buf=db)
            self.transpose_split(ya, yb, 128 * tt, chunk_stride=512)
        wbufs1 = {0: self.wload("w1", 0, 0, 1024)}
        for hc in range(32):
            slot = hc % 2
            if hc + 1 < 32:
                wbufs1[hc + 1] = self.wload("w1", hc + 1, 1024 * ((hc + 1) % 2), 1024)
            bW1 = wbufs1.pop(hc)
            for fc in range(8):
                self.mm3(self.bank(slot),
                         WH[:, 1024 * slot + 128 * fc:1024 * slot + 128 * (fc + 1)],
                         WL[:, 1024 * slot + 128 * fc:1024 * slot + 128 * (fc + 1)],
                         PH[:, 512 * fc:512 * (fc + 1)], PL[:, 512 * fc:512 * (fc + 1)],
                         bW1 + bXT, self.pb(slot), start=(fc == 0), stop=(fc == 7))
            xa, xb = self.fb(3 * slot + 0)
            hseg = self.pseg(4096 + 512 * hc, 4096 + 512 * (hc + 1))
            p.op("act", lambda e, slot=slot, xa=xa, hc=hc: e.activation(
                out=xa[:, 0:512], in_=self.bank(slot), func=AF.Relu,
                bias=self.B1S[:, hc:hc + 1]),
                r=self.pb(slot) + [self.B("b1s")], w=[xb])
            p.op("act", lambda e, xa=xa, hc=hc: e.copy(
                PH[:, 4096 + 512 * hc:4096 + 512 * (hc + 1)], xa[:, 0:512]),
                r=[xb], w=hseg)
            p.op("dve", lambda e, xa=xa, hc=hc: e.tensor_sub(
                PL[:, 4096 + 512 * hc:4096 + 512 * (hc + 1)], xa[:, 0:512],
                PH[:, 4096 + 512 * hc:4096 + 512 * (hc + 1)]),
                r=[xb] + hseg, w=hseg)
        wbufs2 = {0: self.wload("w2", 0, 2048, 1024)}
        for hc in range(32):
            slot = 2 + hc % 2
            if hc + 1 < 32:
                wbufs2[hc + 1] = self.wload("w2", hc + 1, 2048 + 1024 * ((hc + 1) % 2), 1024)
            bW2 = wbufs2.pop(hc)
            hseg = self.pseg(4096 + 512 * hc, 4096 + 512 * (hc + 1))
            for tt in range(TPC):
                hcol = 4096 + 512 * hc + 128 * tt
                jobs = [(self.bank(2 * tt + sx),
                         WH[:, 1024 * slot + 512 * sx:1024 * slot + 512 * (sx + 1)],
                         WL[:, 1024 * slot + 512 * sx:1024 * slot + 512 * (sx + 1)],
                         hc == 0, False, self.pb(2 * tt + sx)) for sx in range(2)]
                self.mmg(PH[:, hcol:hcol + 128], PL[:, hcol:hcol + 128], jobs, bW2 + hseg)
        bB = self.wload("b2", 0, 4096, 1024, prows=1)
        for tt in range(TPC):
            for sx in range(2):
                p.op("pe", lambda e, tt=tt, sx=sx: e.matmul(
                    self.bank(2 * tt + sx), self.ONE[:],
                    WH[:1, 4096 + 512 * sx:4096 + 512 * (sx + 1)],
                    start=False, stop=False), r=[self.B("one")] + bB,
                    w=self.pb(2 * tt + sx))
                p.op("pe", lambda e, tt=tt, sx=sx: e.matmul(
                    self.bank(2 * tt + sx), self.ONE[:],
                    WL[:1, 4096 + 512 * sx:4096 + 512 * (sx + 1)],
                    start=False, stop=True), r=[self.B("one")] + bB,
                    w=self.pb(2 * tt + sx))
            self.epilogue(self.bank(2 * tt, 2), self.pb(2 * tt, 2), b, TPC * ci + tt)

    def s3_l1attn(self, b, ci):
        p = self.p
        PH, PL, FB, FA, PS, WH, WL = self.PH, self.PL, self.FB, self.FA, self.PS, self.WH, self.WL
        self.keepalive(0, 48)
        SEG0 = self.pseg(0, 4096)
        for j in range(TPC):
            bX = self.B(f"xch{j}")
            pt = j % 2
            ya, yb = self.fb(3 * pt + 1)
            da, db = self.fb(3 * pt + 2)
            self.ln_stats_apply(self.XCH[:, 1024 * j:1024 * (j + 1)], bX,
                                64, True, ya, yb, dump_ap=da, dump_buf=db)
            self.transpose_split(ya, yb, 1024 * j, chunk_stride=128)

        def headsplit(j, pbank, dst_c, dst_bufs, to_pairs, src_fb):
            rowo = 64 * (j % 2)
            sap, sbuf = self.fb(src_fb)
            for hb in range(4):
                bk = pbank + (hb % 2 if pbank % 2 == 0 else 0)
                if pbank % 2 == 0:
                    bk = pbank + hb % 2
                else:
                    bk = pbank if hb % 2 == 0 else pbank - 1
                base = 512 * bk
                pbt = self.pb(bk)
                for c in range(4):
                    h = 4 * hb + c
                    p.op("pe", lambda e, h=h, c=c, base=base, sap=sap: e.transpose(
                        PS[0:64, base + 128 * c:base + 128 * (c + 1)],
                        sap[:, 64 * h:64 * (h + 1)], self.IDN[:]),
                        r=[sbuf, self.B("ident")], w=pbt)
                src = PS[0:64, base:base + 512].rearrange("p (c t) -> p c t", t=128)
                if to_pairs:
                    dh = PH[rowo:rowo + 64, dst_c:dst_c + 2048].rearrange(
                        "p (t c) -> p c t", c=16)[:, 4 * hb:4 * hb + 4, :]
                    dl = PL[rowo:rowo + 64, dst_c:dst_c + 2048].rearrange(
                        "p (t c) -> p c t", c=16)[:, 4 * hb:4 * hb + 4, :]
                    p.op("act", lambda e, src=src, dh=dh: e.copy(dh, src),
                         r=pbt, w=dst_bufs)
                    p.op("dve", lambda e, src=src, dh=dh, dl=dl:
                         e.tensor_sub(dl, src, dh), r=pbt + dst_bufs, w=dst_bufs)
                else:
                    dst = FB[0:64, 2048:4096].rearrange(
                        "p (t c) -> p c t", c=16)[:, 4 * hb:4 * hb + 4, :]
                    p.op("dve", lambda e, src=src, dst=dst: e.tensor_copy(dst, src),
                         r=pbt, w=[self.fb(2)[1], self.fb(3)[1]])

        for wi, wname in ((0, "wq1"), (1, "wk1")):
            wbqk = {0: self.wload(wname, 0, 0, 1024)}
            for fc in range(8):
                slot = fc % 2
                if fc + 1 < 8:
                    wbqk[fc + 1] = self.wload(wname, fc + 1, 1024 * ((fc + 1) % 2), 1024)
                bW = wbqk.pop(fc)
                for j in range(TPC):
                    jobs = [(self.bank(2 * j + sx),
                             WH[:, 1024 * slot + 512 * sx:1024 * slot + 512 * (sx + 1)],
                             WL[:, 1024 * slot + 512 * sx:1024 * slot + 512 * (sx + 1)],
                             fc == 0, fc == 7, self.pb(2 * j + sx)) for sx in range(2)]
                    self.mmg(PH[:, 1024 * j + 128 * fc:1024 * j + 128 * (fc + 1)],
                             PL[:, 1024 * j + 128 * fc:1024 * j + 128 * (fc + 1)],
                             jobs, bW + SEG0)
            for j in range(TPC):
                dst_c = 4096 + 4096 * (j // 2) + 2048 * wi
                dstb = self.pseg(dst_c, dst_c + 2048)
                sfb = 4 + j % 2
                sap, sbuf = self.fb(sfb)
                p.op("dve", lambda e, j=j, sap=sap: e.tensor_copy(
                    sap, self.bank(2 * j, 2)), r=self.pb(2 * j, 2), w=[sbuf])
                headsplit(j, 2 * j, dst_c, dstb, True, sfb)

        def vproj(j):
            for sx in range(2):
                for fc in range(8):
                    slot = fc % 2
                    bW = self.wload("wv1", fc, 1024 * slot, 1024)
                    self.mm3(self.bank(7),
                             PH[:, 1024 * j + 128 * fc:1024 * j + 128 * (fc + 1)],
                             PL[:, 1024 * j + 128 * fc:1024 * j + 128 * (fc + 1)],
                             WH[:, 1024 * slot + 512 * sx:1024 * slot + 512 * (sx + 1)],
                             WL[:, 1024 * slot + 512 * sx:1024 * slot + 512 * (sx + 1)],
                             bW + SEG0, self.pb(7), start=(fc == 0), stop=(fc == 7))
                p.op("dve", lambda e, sx=sx: e.tensor_copy(
                    FB[:, 4096 + 512 * sx:4096 + 512 * (sx + 1)], self.bank(7)),
                    r=self.pb(7), w=[self.fb(4)[1]])

        def vtail(j):
            headsplit(j, 4, 0, None, False, 4)
            for wv in range(2):
                bk = 4 + wv % 2
                pbt = self.pb(bk)
                for c in range(8):
                    cc = 8 * wv + c
                    p.op("pe", lambda e, cc=cc, c=c, bk=bk: e.transpose(
                        PS[:, 512 * bk + 64 * c:512 * bk + 64 * (c + 1)],
                        FB[0:64, 2048 + 128 * cc:2048 + 128 * (cc + 1)],
                        self.IDN[0:64, 0:64]),
                        r=[self.fb(2)[1], self.fb(3)[1], self.B("ident")], w=pbt)
                src = PS[:, 512 * bk:512 * bk + 512].rearrange("p (c d) -> p c d", d=64)
                for half in range(2):
                    dst = FB[:, 0:2048].rearrange("p (c d) -> p c d", d=128)[
                        :, 8 * wv:8 * wv + 8, 64 * half:64 * half + 64]
                    p.op("dve", lambda e, src=src, dst=dst:
                         e.tensor_copy(dst, src), r=pbt,
                         w=[self.fb(0)[1], self.fb(1)[1]])

        vproj(0)
        vtail(0)
        for j in range(TPC):
            self._s3_attn_core(b, ci, j)
            if j + 1 < TPC:
                vproj(j + 1)
                vtail(j + 1)
        wbp1 = {0: self.wload("wpr1", 0, 0, 1024)}
        for fc in range(8):
            slot = fc % 2
            if fc + 1 < 8:
                wbp1[fc + 1] = self.wload("wpr1", fc + 1, 1024 * ((fc + 1) % 2), 1024)
            bWp = wbp1.pop(fc)
            for j in range(TPC):
                aoc = 12288 + 1024 * j
                bSrc = self.pseg(aoc, aoc + 1024)
                jobs = [(self.bank(2 * j + sx),
                         WH[:, 1024 * slot + 512 * sx:1024 * slot + 512 * (sx + 1)],
                         WL[:, 1024 * slot + 512 * sx:1024 * slot + 512 * (sx + 1)],
                         fc == 0, False, self.pb(2 * j + sx)) for sx in range(2)]
                self.mmg(PH[:, aoc + 128 * fc:aoc + 128 * (fc + 1)],
                         PL[:, aoc + 128 * fc:aoc + 128 * (fc + 1)],
                         jobs, bWp + bSrc)
        bB = self.wload("bpr1", 0, 2048, 1024, prows=1)
        for j in range(TPC):
            for sx in range(2):
                p.op("pe", lambda e, j=j, sx=sx: e.matmul(
                    self.bank(2 * j + sx), self.ONE[:], WH[:1, 2048 + 512 * sx:2048 + 512 * (sx + 1)],
                    start=False, stop=False), r=[self.B("one")] + bB, w=self.pb(2 * j + sx))
                p.op("pe", lambda e, j=j, sx=sx: e.matmul(
                    self.bank(2 * j + sx), self.ONE[:], WL[:1, 2048 + 512 * sx:2048 + 512 * (sx + 1)],
                    start=False, stop=True), r=[self.B("one")] + bB, w=self.pb(2 * j + sx))
            self.epilogue(self.bank(2 * j, 2), self.pb(2 * j, 2), b, TPC * ci + j)

    def _s3_attn_core(self, b, ci, j):
        p = self.p
        PH, PL, FB, FA, PS, WH, WL = self.PH, self.PL, self.FB, self.FA, self.PS, self.WH, self.WL
        rowo = 64 * (j % 2)
        QTc = 4096 + 4096 * (j // 2)
        KTc = QTc + 2048
        bQT = self.pseg(QTc, QTc + 2048)
        bKT = self.pseg(KTc, KTc + 2048)
        bAe = self.faseg(0, 2048)
        bAo = [self.fb(2)[1], self.fb(3)[1]]
        bO = self.faseg(2048, 4096)
        bAT = self.faseg(4096, 6144)
        bVD = [self.fb(0)[1], self.fb(1)[1]]
        SM0 = self.smseg(0, 16)
        tp = (rowo, 0) if rowo else None

        APAR = [(FA[:, 0:2048], bAe), (FB[:, 2048:4096], bAo),
                (FB[:, 6144:8192], [self.fb(6)[1], self.fb(7)[1]])]

        def at_av(i):
            Aap, bA = APAR[i % 3]
            for c0 in range(0, i + 1, 4):
                cn = min(4, i + 1 - c0)
                bk = 4 + (c0 // 4) % 2
                pbt = self.pb(bk)
                for c in range(cn):
                    p.op("pe", lambda e, c=c, c0=c0, Aap=Aap, bk=bk: e.transpose(
                        PS[:, 512 * bk + 128 * c:512 * bk + 128 * (c + 1)],
                        Aap[:, 128 * (c0 + c):128 * (c0 + c + 1)], self.IDN[:]),
                        r=bA + [self.B("ident")], w=pbt)
                p.op("dve", lambda e, c0=c0, cn=cn, bk=bk: e.tensor_copy(
                    FA[:, 4096 + 128 * c0:4096 + 128 * (c0 + cn)],
                    PS[:, 512 * bk:512 * bk + 128 * cn]), r=pbt, w=bAT)
            ob = 6 + i % 2
            for c in range(i + 1):
                p.op("pe", lambda e, c=c, i=i, ob=ob: e.matmul(
                    self.bank(ob)[:, 0:128], FB[:, 128 * c:128 * (c + 1)],
                    FA[:, 4096 + 128 * c:4096 + 128 * (c + 1)],
                    start=(c == 0), stop=(c == i)), r=bVD + bAT, w=self.pb(ob))
            p.op("dve", lambda e, i=i, ob=ob: e.tensor_copy(
                FA[:, 2048 + 128 * i:2048 + 128 * (i + 1)], self.bank(ob)[:, 0:128]),
                r=self.pb(ob), w=bO)

        for i in range(16):
            if i > 1:
                at_av(i - 2)
            nk = 128 * (i + 1)
            nb = (nk + 511) // 512
            qh = PH[rowo:rowo + 64, QTc + 128 * i:QTc + 128 * (i + 1)]
            ql = PL[rowo:rowo + 64, QTc + 128 * i:QTc + 128 * (i + 1)]
            jobs = []
            for c0 in range(0, nk, 512):
                cw = min(512, nk - c0)
                jobs.append((PS[:, c0:c0 + cw],
                             PH[rowo:rowo + 64, KTc + c0:KTc + c0 + cw],
                             PL[rowo:rowo + 64, KTc + c0:KTc + c0 + cw],
                             True, True, self.pb(c0 // 512)))
            self.mmg(qh, ql, jobs, bQT + bKT, tp=tp)
            p.op("dve", lambda e, nk=nk: e.tensor_tensor(
                PS[:, nk - 128:nk], PS[:, nk - 128:nk], self.MSK[:], ALU.add),
                r=self.pb((nk - 128) // 512) + [self.B("mask")], w=self.pb((nk - 128) // 512))
            p.op("dve", lambda e, nk=nk: e.reduce_max(
                self.SM[:, 0:1], PS[:, 0:nk], axis=AX.X, negate=True),
                r=self.pb(0, nb), w=[SM0[0]])
            p.op("dve", lambda e: e.tensor_scalar(self.SM[:, 1:2], self.SM[:, 0:1],
                                                  0.125, None, ALU.mult),
                 r=[SM0[0]], w=[SM0[0]])
            Aap, bA = APAR[i % 3]
            p.op("act", lambda e, nk=nk, Aap=Aap: e.activation(
                out=Aap[:, 0:nk], in_=PS[:, 0:nk], func=AF.Exp,
                bias=self.SM[:, 1:2], scale=self.CST[:, 0:1], accum_out=self.SM[:, 2:3]),
                r=self.pb(0, nb) + [SM0[0], self.B("cst")], w=bA + [SM0[0]])
            p.op("dve", lambda e: e.reciprocal(self.SM[:, 3:4], self.SM[:, 2:3]),
                 r=[SM0[0]], w=[SM0[0]])
            p.op("act", lambda e, nk=nk, Aap=Aap: e.activation(
                out=Aap[:, 0:nk], in_=Aap[:, 0:nk], func=AF.Copy,
                scale=self.SM[:, 3:4]), r=bA + [SM0[0]], w=bA)
        at_av(14)
        at_av(15)
        aoc = 12288 + 1024 * j
        bO = self.faseg(2048, 4096)
        bAO = self.pseg(aoc, aoc + 1024)
        for c in range(8):
            for half in range(2):
                rr = slice(64 * half, 64 * (half + 1))
                src = FA[rr, 2048 + 2 * c + half:2048 + 2 * c + half + 16 * 127 + 1:16]
                dh = PH[rr, aoc + 128 * c:aoc + 128 * (c + 1)]
                dl = PL[rr, aoc + 128 * c:aoc + 128 * (c + 1)]
                p.op("act", lambda e, src=src, dh=dh: e.copy(dh, src), r=bO, w=bAO)
                p.op("dve", lambda e, src=src, dh=dh, dl=dl: e.tensor_sub(dl, src, dh),
                     r=bO + bAO, w=bAO)

    def s4_moe(self, b, ci):
        p = self.p
        PH, PL, FB, FA, PS, WH, WL = self.PH, self.PL, self.FB, self.FA, self.PS, self.WH, self.WL
        self.keepalive(6, 48)
        SEG0 = self.pseg(0, 4096)
        SM0 = self.smseg(0, 16)
        bFA = self.faseg(0, 1024)[0]
        bWT = self.B("wtp")
        for tt in range(TPC):
            bX = self.B(f"xch{tt}")
            pt = tt % 2
            ya, yb = self.fb(3 * pt + 1)
            da, db = self.fb(3 * pt + 2)
            self.ln_stats_apply(self.XCH[:, 1024 * tt:1024 * (tt + 1)], bX,
                                64, True, ya, yb, dump_ap=da, dump_buf=db)
            self.transpose_split(ya, yb, 128 * tt, chunk_stride=512)
        bB = self.wload("brt", 0, 2048, 32, prows=1)
        for tt in range(TPC):
            bk = 4 + tt
            for fc in range(8):
                bW = self.wload("wrt", fc, 1024 * (fc % 2), 32)
                self.mm3(PS[:, 512 * bk:512 * bk + 32],
                         PH[:, 512 * fc + 128 * tt:512 * fc + 128 * (tt + 1)],
                         PL[:, 512 * fc + 128 * tt:512 * fc + 128 * (tt + 1)],
                         WH[:, 1024 * (fc % 2):1024 * (fc % 2) + 32],
                         WL[:, 1024 * (fc % 2):1024 * (fc % 2) + 32],
                         bW + SEG0, self.pb(bk), start=(fc == 0), stop=False)
            p.op("pe", lambda e, bk=bk: e.matmul(PS[:, 512 * bk:512 * bk + 32], self.ONE[:],
                                                 WH[:1, 2048:2048 + 32], start=False,
                                                 stop=False), r=[self.B("one")] + bB,
                 w=self.pb(bk))
            p.op("pe", lambda e, bk=bk: e.matmul(PS[:, 512 * bk:512 * bk + 32], self.ONE[:],
                                                 WL[:1, 2048:2048 + 32], start=False,
                                                 stop=True), r=[self.B("one")] + bB,
                 w=self.pb(bk))
        for tt in range(TPC):
            bk = 4 + tt
            sm = self.SM
            fo = 128 * tt
            smx = FA[:, fo:fo + 32]
            wk = FA[:, fo + 32:fo + 64]
            msk = FA[:, fo + 64:fo + 96]
            w_ = FA[:, fo + 96:fo + 128]
            lg = PS[:, 512 * bk:512 * bk + 32]
            p.op("dve", lambda e, lg=lg: e.reduce_max(sm[:, 0:1], lg, axis=AX.X,
                                                      negate=True), r=self.pb(bk), w=[SM0[0]])
            p.op("act", lambda e, lg=lg, smx=smx: e.activation(
                out=smx, in_=lg, func=AF.Exp, bias=sm[:, 0:1], accum_out=sm[:, 1:2]),
                r=self.pb(bk) + [SM0[0]], w=[bFA, SM0[0]])
            p.op("dve", lambda e: e.reciprocal(sm[:, 2:3], sm[:, 1:2]), r=[SM0[0]], w=[SM0[0]])
            p.op("dve", lambda e, smx=smx: e.tensor_scalar(smx, smx, sm[:, 2:3], None, ALU.mult),
                 r=[bFA, SM0[0]], w=[bFA])
            p.op("dve", lambda e, wk=wk, smx=smx: e.tensor_copy(wk, smx), r=[bFA], w=[bFA])
            for it in range(3):
                p.op("dve", lambda e, wk=wk: e.reduce_max(sm[:, 3:4], wk, axis=AX.X),
                     r=[bFA], w=[SM0[0]])
                p.op("dve", lambda e, wk=wk, msk=msk: e.tensor_scalar(
                    msk, wk, sm[:, 3:4], None, ALU.is_lt), r=[bFA, SM0[0]], w=[bFA])
                p.op("dve", lambda e, wk=wk, msk=msk: e.tensor_mul(wk, wk, msk),
                     r=[bFA], w=[bFA])
            p.op("dve", lambda e, wk=wk: e.reduce_max(sm[:, 4:5], wk, axis=AX.X),
                 r=[bFA], w=[SM0[0]])
            p.op("dve", lambda e, smx=smx, msk=msk: e.tensor_scalar(
                msk, smx, sm[:, 4:5], None, ALU.is_ge), r=[bFA, SM0[0]], w=[bFA])
            p.op("dve", lambda e, w_=w_, smx=smx, msk=msk: e.tensor_mul(w_, smx, msk),
                 r=[bFA], w=[bFA])
            p.op("dve", lambda e, w_=w_: e.reduce_sum(sm[:, 5:6], w_, axis=AX.X),
                 r=[bFA], w=[SM0[0]])
            p.op("dve", lambda e: e.tensor_scalar(sm[:, 6:7], sm[:, 5:6], 1e-12, None,
                                                  ALU.max), r=[SM0[0]], w=[SM0[0]])
            p.op("dve", lambda e: e.reciprocal(sm[:, 7:8], sm[:, 6:7]), r=[SM0[0]], w=[SM0[0]])
            p.op("dve", lambda e, w_=w_: e.tensor_scalar(w_, w_, sm[:, 7:8], None, ALU.mult),
                 r=[bFA, SM0[0]], w=[bFA])
            pbt = self.pb(bk)
            p.op("pe", lambda e, w_=w_, bk=bk: e.transpose(
                PS[0:32, 512 * bk + 128:512 * bk + 256], w_, self.IDN[:]),
                r=[bFA, self.B("ident")], w=pbt)
            p.op("act", lambda e, tt=tt, bk=bk: e.copy(
                self.WTP[0:32, 128 * tt:128 * (tt + 1)],
                PS[0:32, 512 * bk + 128:512 * bk + 256]), r=pbt, w=[bWT])
            p.op("dve", lambda e, tt=tt, bk=bk: e.tensor_sub(
                self.WTP[0:32, 512 + 128 * tt:512 + 128 * (tt + 1)],
                PS[0:32, 512 * bk + 128:512 * bk + 256],
                self.WTP[0:32, 128 * tt:128 * (tt + 1)]), r=pbt + [bWT], w=[bWT])
        for sc in range(4):
            pk = 2 * (sc % 2)
            bG = self.wload("wsg", sc, 0, 1024)
            bU = self.wload("wsu", sc, 1024, 1024)
            for fc in range(8):
                self.mm3(self.bank(pk), WH[:, 128 * fc:128 * (fc + 1)],
                         WL[:, 128 * fc:128 * (fc + 1)],
                         PH[:, 512 * fc:512 * (fc + 1)], PL[:, 512 * fc:512 * (fc + 1)],
                         bG + SEG0, self.pb(pk), start=(fc == 0), stop=(fc == 7))
                self.mm3(self.bank(pk + 1), WH[:, 1024 + 128 * fc:1024 + 128 * (fc + 1)],
                         WL[:, 1024 + 128 * fc:1024 + 128 * (fc + 1)],
                         PH[:, 512 * fc:512 * (fc + 1)], PL[:, 512 * fc:512 * (fc + 1)],
                         bU + SEG0, self.pb(pk + 1), start=(fc == 0), stop=(fc == 7))
            xa, xb = self.fb(3 * (sc % 2) + 0)
            p.op("act", lambda e, pk=pk, xa=xa: e.activation(out=xa[:, 0:512], in_=self.bank(pk),
                                                             func=AF.Silu), r=self.pb(pk), w=[xb])
            p.op("dve", lambda e, pk=pk, xa=xa: e.tensor_mul(xa[:, 512:1024], xa[:, 0:512],
                                                             self.bank(pk + 1)),
                 r=[xb] + self.pb(pk + 1), w=[xb])
            hsb = self.wseg(4096 + 512 * sc, 4096 + 512 * (sc + 1))
            p.op("act", lambda e, sc=sc, xa=xa: e.copy(WH[:, 4096 + 512 * sc:4096 + 512 * (sc + 1)],
                                                       xa[:, 512:1024]), r=[xb], w=hsb)
            p.op("dve", lambda e, sc=sc, xa=xa: e.tensor_sub(
                WL[:, 4096 + 512 * sc:4096 + 512 * (sc + 1)], xa[:, 512:1024],
                WH[:, 4096 + 512 * sc:4096 + 512 * (sc + 1)]), r=[xb] + hsb, w=hsb)
        wbg = {0: self.wload("wg", 0, 0, 1024)}
        wbu = {0: self.wload("wu", 0, 2048, 1024)}
        for e_ in range(32):
            slot = e_ % 2
            pk = 3 * slot
            if e_ + 1 < 32:
                wbg[e_ + 1] = self.wload("wg", e_ + 1, 1024 * ((e_ + 1) % 2), 1024)
                wbu[e_ + 1] = self.wload("wu", e_ + 1, 2048 + 1024 * ((e_ + 1) % 2), 1024)
            bG = wbg.pop(e_)
            bU = wbu.pop(e_)
            for fc in range(8):
                self.mm3(self.bank(pk), WH[:, 1024 * slot + 128 * fc:1024 * slot + 128 * (fc + 1)],
                         WL[:, 1024 * slot + 128 * fc:1024 * slot + 128 * (fc + 1)],
                         PH[:, 512 * fc:512 * (fc + 1)], PL[:, 512 * fc:512 * (fc + 1)],
                         bG + SEG0, self.pb(pk), start=(fc == 0), stop=(fc == 7))
                self.mm3(self.bank(pk + 1), WH[:, 2048 + 1024 * slot + 128 * fc:2048 + 1024 * slot + 128 * (fc + 1)],
                         WL[:, 2048 + 1024 * slot + 128 * fc:2048 + 1024 * slot + 128 * (fc + 1)],
                         PH[:, 512 * fc:512 * (fc + 1)], PL[:, 512 * fc:512 * (fc + 1)],
                         bU + SEG0, self.pb(pk + 1), start=(fc == 0), stop=(fc == 7))
            p.op("pe", lambda e, e_=e_, pk=pk: e.matmul(
                self.bank(pk + 2), self.ESEL[:, 128 * e_:128 * (e_ + 1)],
                self.WTP[0:32, 0:512],
                start=True, stop=False), r=[self.B("esel"), bWT], w=self.pb(pk + 2))
            p.op("pe", lambda e, e_=e_, pk=pk: e.matmul(
                self.bank(pk + 2), self.ESEL[:, 128 * e_:128 * (e_ + 1)],
                self.WTP[0:32, 512:1024],
                start=False, stop=True), r=[self.B("esel"), bWT], w=self.pb(pk + 2))
            xa, xb = self.fb(3 * slot + 0)
            ya, yb = self.fb(3 * slot + 1)
            p.op("act", lambda e, pk=pk, xa=xa: e.activation(out=xa[:, 0:512], in_=self.bank(pk),
                                                             func=AF.Silu), r=self.pb(pk), w=[xb])
            p.op("dve", lambda e, pk=pk, xa=xa: e.tensor_mul(xa[:, 512:1024], xa[:, 0:512],
                                                             self.bank(pk + 2)),
                 r=[xb] + self.pb(pk + 2), w=[xb])
            p.op("dve", lambda e, pk=pk, xa=xa, ya=ya: e.tensor_mul(ya[:, 0:512], xa[:, 512:1024],
                                                                    self.bank(pk + 1)),
                 r=[xb] + self.pb(pk + 1), w=[yb])
            hb = self.pseg(4096 + 512 * e_, 4096 + 512 * (e_ + 1))
            p.op("act", lambda e, e_=e_, ya=ya: e.copy(PH[:, 4096 + 512 * e_:4096 + 512 * (e_ + 1)],
                                                       ya[:, 0:512]), r=[yb], w=hb)
            p.op("dve", lambda e, e_=e_, ya=ya: e.tensor_sub(
                PL[:, 4096 + 512 * e_:4096 + 512 * (e_ + 1)], ya[:, 0:512],
                PH[:, 4096 + 512 * e_:4096 + 512 * (e_ + 1)]), r=[yb] + hb, w=hb)
        wbd = {0: self.wload("wd", 0, 0, 1024)}
        for e_ in range(32):
            slot = e_ % 2
            if e_ + 1 < 32:
                wbd[e_ + 1] = self.wload("wd", e_ + 1, 1024 * ((e_ + 1) % 2), 1024)
            bW = wbd.pop(e_)
            hb = self.pseg(4096 + 512 * e_, 4096 + 512 * (e_ + 1))
            for tt in range(TPC):
                hcol = 4096 + 512 * e_ + 128 * tt
                jobs = [(self.bank(2 * tt + sx),
                         WH[:, 1024 * slot + 512 * sx:1024 * slot + 512 * (sx + 1)],
                         WL[:, 1024 * slot + 512 * sx:1024 * slot + 512 * (sx + 1)],
                         e_ == 0, False, self.pb(2 * tt + sx)) for sx in range(2)]
                self.mmg(PH[:, hcol:hcol + 128], PL[:, hcol:hcol + 128], jobs, bW + hb)
        for sc in range(4):
            slot = 2 + sc % 2
            bW = self.wload("wsd", sc, 1024 * slot, 1024)
            hsb = self.wseg(4096 + 512 * sc, 4096 + 512 * (sc + 1))
            for tt in range(TPC):
                hcol = 4096 + 512 * sc + 128 * tt
                jobs = [(self.bank(2 * tt + sx),
                         WH[:, 1024 * slot + 512 * sx:1024 * slot + 512 * (sx + 1)],
                         WL[:, 1024 * slot + 512 * sx:1024 * slot + 512 * (sx + 1)],
                         False, sc == 3, self.pb(2 * tt + sx)) for sx in range(2)]
                self.mmg(WH[:, hcol:hcol + 128], WL[:, hcol:hcol + 128], jobs, bW + hsb)
        for tt in range(TPC):
            self.epilogue(self.bank(2 * tt, 2), self.pb(2 * tt, 2), b, TPC * ci + tt)

    def sload_only(self, b, ci):
        p = self.p
        for tt in range(TPC):
            t_abs = TPC * ci + tt
            bX = self.B(f"xch{tt}")
            p.op("sp", lambda e, t_abs=t_abs, tt=tt: e.dma_start(
                out=self.XCH[:, 1024 * tt:1024 * (tt + 1)],
                in_=self.x_in[b, 128 * t_abs:128 * (t_abs + 1), :]), w=[bX])

    def build(self, stages="A1234"):
        self.declare()
        self.alloc()
        self.p.alloc_sems()
        self.presplit()
        if "A" in stages:
            for b in range(BLOC):
                self.phase_a(b)
        for b in range(BLOC):
            for ci in range(NCHUNK):
                if "1" in stages:
                    self.s1_l0attn(b, ci)
                else:
                    self.sload_only(b, ci)
                if "2" in stages:
                    self.s2_ffn(b, ci)
                if "3" in stages:
                    self.s3_l1attn(b, ci)
                if "4" in stages:
                    self.s4_moe(b, ci)
                self.s5_store(b, ci)
        self.p.emit()
        self.close_ctx()
        return self.nc


DEBUG_STAGES = "A1234"
DEBUG_CORES = NCORE


def kernel(**inputs):
    g, shards = host_prep(inputs)
    shapes = {k: np.asarray(v).shape for k, v in g.items()}
    kb = KB(shapes)
    nc = kb.build(DEBUG_STAGES)
    in_maps = []
    for c in range(DEBUG_CORES):
        m = {"x": shards[c]}
        for name in PAIR_WEIGHTS + RAW_WEIGHTS + ["ident", "mask", "esel"]:
            m[name] = g[name]
        in_maps.append(m)
    res = run_bass_kernel_spmd(nc, in_maps, core_ids=list(range(DEBUG_CORES)))
    return np.concatenate([r["out"] for r in res.results], axis=0)
